# version 02 (final)
# baseline (speedup 1.0000x reference)
import numpy as np
import ml_dtypes
import concourse.bass as bass
import concourse.mybir as mybir
from concourse.bass_utils import run_bass_kernel_spmd

F32 = mybir.dt.float32
BF16 = mybir.dt.bfloat16
I32 = mybir.dt.int32
AX = mybir.AxisListType.X
OP = mybir.AluOpType
AF = mybir.ActivationFunctionType
BF = ml_dtypes.bfloat16

B, T, D = 2, 8192, 1024
NTOK = B * T
NC = 8
NT = NTOK // NC
TIL = NT // 128
EPOCH = 40000
NEG = -3.0e38

TRACE = None


class P:
    def __init__(self):
        self.nc = bass.Bass("TRN2", target_bir_lowering=False)
        self.ops = []
        self._guards = []

    def sbuf(self, name, shape, dtype):
        g = self.nc.sbuf_tensor(name, list(shape), dtype)
        t = g.__enter__()
        self._guards.append(g)
        return t

    def psum(self, name, shape, dtype=F32):
        g = self.nc.psum_tensor(name, list(shape), dtype)
        t = g.__enter__()
        self._guards.append(g)
        return t

    def inp(self, name, shape, dtype):
        return self.nc.dram_tensor(name, list(shape), dtype, kind="ExternalInput").ap()

    def outp(self, name, shape, dtype):
        return self.nc.dram_tensor(name, list(shape), dtype, kind="ExternalOutput").ap()

    def emit(self, eng, fn, wait=True, inc=1):
        self.ops.append((eng, fn, wait, inc))

    def dma(self, dst, src, eng="sync", **kw):
        self.emit(eng, lambda e: e.dma_start(dst, src, **kw), inc=16)

    def group(self, eng, fns):
        n = len(fns)
        for i, fn in enumerate(fns):
            self.emit(eng, fn, wait=(i == 0), inc=(1 if i == n - 1 else 0))

    def build(self):
        nc = self.nc
        sched = []
        epoch = 0
        cum = 0
        totals = []
        for (eng, fn, wait, inc) in self.ops:
            if cum >= EPOCH and wait:
                totals.append(cum)
                epoch += 1
                cum = 0
            sched.append((eng, fn, wait, inc, epoch, cum))
            cum += inc
        totals.append(cum)
        sems = []
        for i in range(epoch + 1):
            g = nc.semaphore(f"gs{i}")
            sems.append(g.__enter__())
            self._guards.append(g)
        blk_g = nc.Block()
        block = blk_g.__enter__()

        def mkbody(nm):
            def body(engobj):
                cur = 0
                for (eng, fn, wait, inc, ep, cumv) in sched:
                    if eng != nm:
                        continue
                    while cur < ep:
                        engobj.wait_ge(sems[cur], totals[cur])
                        cur += 1
                    if wait and cumv > 0:
                        engobj.wait_ge(sems[ep], cumv)
                    inst = fn(engobj)
                    if inc:
                        inst.then_inc(sems[ep], inc)
                if nm == "sync":
                    while cur < epoch:
                        engobj.wait_ge(sems[cur], totals[cur])
                        cur += 1
                    engobj.wait_ge(sems[epoch], totals[epoch])
            return body

        used = set(e for (e, _, _, _) in self.ops)
        used.add("sync")
        for nm in ["sync", "tensor", "vector", "scalar", "gpsimd"]:
            if nm in used:
                getattr(block, nm)(mkbody(nm))
        blk_g.__exit__(None, None, None)
        for g in reversed(self._guards):
            g.__exit__(None, None, None)
        self._guards = []
        return nc



def ln_stats(p, x, negmu, rstd, scratch, D_, eps=1e-5):
    p.emit("vector", lambda e: e.reduce_sum(negmu[:, 0:1], x[:, :D_], axis=AX))
    p.emit("vector", lambda e: e.tensor_scalar_mul(negmu[:, 0:1], negmu[:, 0:1], -1.0 / D_))
    p.emit("vector", lambda e: e.tensor_scalar_add(scratch[:, :D_], x[:, :D_], negmu[:, 0:1]))
    p.emit("scalar", lambda e: e.activation(scratch[:, :D_], scratch[:, :D_], AF.Square,
                                            accum_out=rstd[:, 0:1]))
    p.emit("vector", lambda e: e.tensor_scalar(rstd[:, 0:1], rstd[:, 0:1], 1.0 / D_, eps,
                                               op0=OP.mult, op1=OP.add))
    p.emit("scalar", lambda e: e.sqrt(rstd[:, 0:1], rstd[:, 0:1]))
    p.emit("vector", lambda e: e.reciprocal(rstd[:, 0:1], rstd[:, 0:1]))


def tr_in(p, src, dst, ptile, ident, kchunks, src_col0=0, dst_col0=0):
    k = 0
    while k < kchunks:
        nb = min(4, kchunks - k)
        for j in range(nb):
            kk = k + j
            p.emit("tensor", lambda e, kk=kk, j=j: e.transpose(
                ptile[:, j * 128:(j + 1) * 128],
                src[:, src_col0 + kk * 128: src_col0 + (kk + 1) * 128],
                ident[:, :]))
        p.emit("vector", lambda e, k=k, nb=nb: e.tensor_copy(
            dst[:, dst_col0 + k * 128: dst_col0 + (k + nb) * 128], ptile[:, :nb * 128]))
        k += nb


def mm_acc(p, out_ps, lhs, rhs, bias=None):
    fns = []
    n = len(lhs)
    if bias is not None:
        ones_ap, brow = bias
        fns.append(lambda e: e.matmul(out_ps, ones_ap, brow, start=True, stop=False))
    for i in range(n):
        st = (i == 0) and (bias is None)
        sp = (i == n - 1)
        fns.append(lambda e, i=i, st=st, sp=sp: e.matmul(out_ps, lhs[i], rhs[i],
                                                         start=st, stop=sp))
    p.group("tensor", fns)


def load_cast_w(p, hbm, stag, dst_bf, kchunks, ncols, dst_col0=0):
    for k in range(kchunks):
        p.dma(stag[:, :ncols], hbm[k * 128:(k + 1) * 128, :ncols])
        p.emit("vector", lambda e, k=k: e.tensor_copy(
            dst_bf[:, dst_col0 + k * ncols: dst_col0 + (k + 1) * ncols], stag[:, :ncols]))


def load_row_bf(p, hbm_row, stag_row, dst_row, n):
    p.dma(stag_row[0:1, :n], hbm_row)
    p.emit("vector", lambda e: e.tensor_copy(dst_row[0:1, :n], stag_row[0:1, :n]))


def bcast_row(p, ones1, brow_bf, pbank, dst, n):
    for c0 in range(0, n, 512):
        w = min(512, n - c0)
        p.group("tensor", [lambda e, c0=c0, w=w: e.matmul(
            pbank[:, :w], ones1[0:1, :], brow_bf[0:1, c0:c0 + w], start=True, stop=True)])
        p.emit("vector", lambda e, c0=c0, w=w: e.tensor_copy(dst[:, c0:c0 + w], pbank[:, :w]))


_progs = {}


def get_prog(name, builder):
    if name not in _progs:
        _progs[name] = builder()
    return _progs[name]


def common_alloc(p):
    d = {}
    d["idb"] = p.sbuf("idb", [128, 128], BF16)
    d["ones1"] = p.sbuf("ones1", [1, 128], BF16)
    idb_in = p.inp("ident", [128, 128], BF16)
    p.dma(d["idb"][:, :], idb_in[:, :])
    p.emit("vector", lambda e: e.memset(d["ones1"][:, :], 1.0))
    return d


def build_S1():
    p = P()
    x = p.inp("x", [NT, D], F32)
    wqT = p.inp("wqT", [D, D], F32)
    wkT = p.inp("wkT", [D, D], F32)
    wvT = p.inp("wvT", [D, D], F32)
    bq = p.inp("bq", [1, D], F32)
    bk = p.inp("bk", [1, D], F32)
    bv = p.inp("bv", [1, D], F32)
    o_qT = p.outp("qT", [D, NT], BF16)
    o_kvp = p.outp("kvp", [D, 65], F32)

    cm = common_alloc(p)
    idb, ones1 = cm["idb"], cm["ones1"]
    stag = p.sbuf("stag", [128, D], F32)
    wq = p.sbuf("wq", [128, 8 * D], BF16)
    wk = p.sbuf("wk", [128, 8 * D], BF16)
    wv = p.sbuf("wv", [128, 8 * D], BF16)
    srow = p.sbuf("srow", [1, D], F32)
    bqb = p.sbuf("bqb", [1, D], BF16)
    bkb = p.sbuf("bkb", [1, D], BF16)
    bvb = p.sbuf("bvb", [1, D], BF16)
    ones128 = p.sbuf("ones128", [128, 1], BF16)
    xt = p.sbuf("xt", [128, D], F32)
    scr = p.sbuf("scr", [128, D], F32)
    st1 = p.sbuf("st1", [128, 2], F32)
    st2 = p.sbuf("st2", [128, 2], F32)
    xnb = p.sbuf("xnb", [128, D], BF16)
    xnT = p.sbuf("xnT", [128, D], BF16)
    qb = p.sbuf("qb", [128, D], BF16)
    kb = p.sbuf("kb", [128, D], BF16)
    vb = p.sbuf("vb", [128, D], BF16)
    tmp1 = p.sbuf("tmp1", [128, 512], F32)
    tmp2 = p.sbuf("tmp2", [128, 512], F32)
    qTsb = p.sbuf("qTsb", [128, 8 * NT], BF16)
    kvacc = p.sbuf("kvacc", [128, 8 * 65], F32)
    pA = p.psum("pA", [128, 512], F32)
    pB = p.psum("pB", [128, 128], F32)
    pT = p.psum("pT", [128, 512], BF16)

    load_cast_w(p, wqT, stag, wq, 8, D)
    load_cast_w(p, wkT, stag, wk, 8, D)
    load_cast_w(p, wvT, stag, wv, 8, D)
    load_row_bf(p, bq[0:1, :], srow, bqb, D)
    load_row_bf(p, bk[0:1, :], srow, bkb, D)
    load_row_bf(p, bv[0:1, :], srow, bvb, D)
    p.emit("vector", lambda e: e.memset(ones128[:, :], 1.0))
    p.emit("vector", lambda e: e.memset(kvacc[:, :], 0.0))

    def elu1(dst_bf):
        def go(c0, w):
            p.emit("vector", lambda e: e.tensor_scalar_min(tmp1[:, :w], pA[:, :w], 0.0))
            p.emit("scalar", lambda e: e.activation(tmp1[:, :w], tmp1[:, :w], AF.Exp))
            p.emit("vector", lambda e: e.tensor_scalar_max(tmp2[:, :w], pA[:, :w], 0.0))
            p.emit("vector", lambda e: e.tensor_add(dst_bf[:, c0:c0 + w], tmp2[:, :w],
                                                    tmp1[:, :w]))
        return go

    for t in range(TIL):
        p.dma(xt[:, :], x[t * 128:(t + 1) * 128, :])
        ln_stats(p, xt, st1, st2, scr, D)
        p.emit("vector", lambda e: e.tensor_scalar(xnb[:, :], xt[:, :], st1[:, 0:1],
                                                   st2[:, 0:1], op0=OP.add, op1=OP.mult))
        tr_in(p, xnb, xnT, pT, idb, 8)
        lhs = [xnT[:, k * 128:(k + 1) * 128] for k in range(8)]
        for (wsb, brow, dst, act) in ((wq, bqb, qb, True), (wk, bkb, kb, True),
                                      (wv, bvb, vb, False)):
            for c0 in range(0, D, 512):
                rhs = [wsb[:, k * D + c0: k * D + c0 + 512] for k in range(8)]
                mm_acc(p, pA[:, :512], lhs, rhs, bias=(ones1[0:1, :], brow[0:1, c0:c0 + 512]))
                if act:
                    elu1(dst)(c0, 512)
                else:
                    p.emit("vector", lambda e, c0=c0, dst=dst: e.tensor_copy(
                        dst[:, c0:c0 + 512], pA[:, :512]))
        for k in range(8):
            p.emit("tensor", lambda e, k=k: e.transpose(pT[:, :128],
                                                        qb[:, k * 128:(k + 1) * 128], idb[:, :]))
            p.emit("vector", lambda e, k=k, t=t: e.tensor_copy(
                qTsb[:, k * NT + t * 128: k * NT + (t + 1) * 128], pT[:, :128]))
        for pr in range(8):
            h0, h1 = 2 * pr, 2 * pr + 1
            p.group("tensor", [
                lambda e, h0=h0: e.matmul(pB[0:64, 0:64], kb[:, h0 * 64:(h0 + 1) * 64],
                                          vb[:, h0 * 64:(h0 + 1) * 64], start=True, stop=True),
                lambda e, h1=h1: e.matmul(pB[64:128, 0:64], kb[:, h1 * 64:(h1 + 1) * 64],
                                          vb[:, h1 * 64:(h1 + 1) * 64], start=True, stop=True),
                lambda e, h0=h0: e.matmul(pB[0:64, 64:65], kb[:, h0 * 64:(h0 + 1) * 64],
                                          ones128[:, :], start=True, stop=True),
                lambda e, h1=h1: e.matmul(pB[64:128, 64:65], kb[:, h1 * 64:(h1 + 1) * 64],
                                          ones128[:, :], start=True, stop=True),
            ])
            p.emit("vector", lambda e, pr=pr: e.tensor_add(
                kvacc[:, pr * 65:(pr + 1) * 65], kvacc[:, pr * 65:(pr + 1) * 65],
                pB[:, 0:65]))
    for k in range(8):
        p.dma(o_qT[k * 128:(k + 1) * 128, :], qTsb[:, k * NT:(k + 1) * NT])
    p.dma(o_kvp.rearrange("(a b) c -> b a c", b=128),
          kvacc.rearrange("b (a c) -> b a c", a=8))
    return p.build()


def build_S2():
    p = P()
    x = p.inp("x", [NT, D], F32)
    qT = p.inp("qT", [D, NT], BF16)
    kvps = [p.inp(f"kvp{i}", [D, 65], F32) for i in range(4)]
    woT = p.inp("woT", [D, D], F32)
    bo = p.inp("bo", [1, D], F32)
    o_x1 = p.outp("x1", [NT, D], F32)

    cm = common_alloc(p)
    idb, ones1 = cm["idb"], cm["ones1"]
    stag = p.sbuf("stag", [128, D], F32)
    wo = p.sbuf("wo", [128, 8 * D], BF16)
    srow = p.sbuf("srow", [1, D], F32)
    bob = p.sbuf("bob", [1, D], BF16)
    qTsb = p.sbuf("qTsb", [128, 8 * NT], BF16)
    kvs = p.sbuf("kvs", [128, 8 * 65], F32)
    kvt = p.sbuf("kvt", [128, 8 * 65], F32)
    kvb = p.sbuf("kvb", [128, 8 * 65], BF16)
    att = p.sbuf("att", [128, D], F32)
    attb = p.sbuf("attb", [128, D], BF16)
    attT = p.sbuf("attT", [128, D], BF16)
    den = p.sbuf("den", [128, 16], F32)
    rden = p.sbuf("rden", [128, 16], F32)
    xt = p.sbuf("xt", [128, D], F32)
    osb = p.sbuf("osb", [128, D], F32)
    pA = p.psum("pA", [128, 512], F32)
    pC = p.psum("pC", [128, 128], F32)
    pT = p.psum("pT", [128, 512], BF16)

    load_cast_w(p, woT, stag, wo, 8, D)
    load_row_bf(p, bo[0:1, :], srow, bob, D)
    p.dma(qTsb.rearrange("b (a c) -> b a c", a=8), qT.rearrange("(a b) c -> b a c", b=128))
    for i in range(4):
        dst = kvs if i == 0 else kvt
        p.dma(dst.rearrange("b (a c) -> b a c", a=8),
              kvps[i].rearrange("(a b) c -> b a c", b=128))
        if i > 0:
            p.emit("vector", lambda e: e.tensor_add(kvs[:, :], kvs[:, :], kvt[:, :]))
    p.emit("vector", lambda e: e.tensor_copy(kvb[:, :], kvs[:, :]))

    for t in range(TIL):
        for h in range(16):
            par, pr = h % 2, h // 2
            p.group("tensor", [lambda e, par=par, pr=pr, t=t: e.matmul(
                pC[:, 0:65],
                qTsb[par * 64:(par + 1) * 64, pr * NT + t * 128: pr * NT + (t + 1) * 128],
                kvb[par * 64:(par + 1) * 64, pr * 65:(pr + 1) * 65],
                start=True, stop=True)])
            p.emit("vector", lambda e, h=h: e.tensor_copy(att[:, h * 64:(h + 1) * 64],
                                                          pC[:, 0:64]))
            p.emit("vector", lambda e, h=h: e.tensor_copy(den[:, h:h + 1], pC[:, 64:65]))
        p.emit("vector", lambda e: e.tensor_scalar_max(den[:, :], den[:, :], 1e-6))
        p.emit("vector", lambda e: e.reciprocal(rden[:, :], den[:, :]))
        for h in range(16):
            p.emit("vector", lambda e, h=h: e.tensor_scalar_mul(
                attb[:, h * 64:(h + 1) * 64], att[:, h * 64:(h + 1) * 64], rden[:, h:h + 1]))
        tr_in(p, attb, attT, pT, idb, 8)
        p.dma(xt[:, :], x[t * 128:(t + 1) * 128, :])
        lhs = [attT[:, k * 128:(k + 1) * 128] for k in range(8)]
        for c0 in range(0, D, 512):
            rhs = [wo[:, k * D + c0: k * D + c0 + 512] for k in range(8)]
            mm_acc(p, pA[:, :512], lhs, rhs, bias=(ones1[0:1, :], bob[0:1, c0:c0 + 512]))
            p.emit("vector", lambda e, c0=c0: e.tensor_add(osb[:, c0:c0 + 512],
                                                           pA[:, :512], xt[:, c0:c0 + 512]))
        p.dma(o_x1[t * 128:(t + 1) * 128, :], osb[:, :])
    return p.build()


def build_S3():
    p = P()
    x = p.inp("x", [NT, D], F32)
    qwT = p.inp("qwT", [6 * D, D], I32)
    screp = p.inp("screp", [6 * D, D], F32)
    biases = p.inp("biases", [6, D], F32)
    laT = p.inp("laT", [6 * D, 32], F32)
    lbT = p.inp("lbT", [6 * 32, D], F32)
    lng = p.inp("lng", [3, D], F32)
    lnb = p.inp("lnb", [3, D], F32)
    o_x2 = p.outp("x2", [NT, D], F32)

    cm = common_alloc(p)
    idb, ones1 = cm["idb"], cm["ones1"]
    qi = p.sbuf("qi", [128, D], I32)
    qf = p.sbuf("qf", [128, D], F32)
    sc = p.sbuf("sc", [128, D], F32)
    wbs = [p.sbuf(f"wb{m}", [128, 8 * D], BF16) for m in range(6)]
    lab = p.sbuf("lab", [128, 6 * 8 * 32], BF16)
    lbb = p.sbuf("lbb", [32, 6 * D], BF16)
    stag = p.sbuf("stag", [128, D], F32)
    srow = p.sbuf("srow", [1, D], F32)
    bias_b = p.sbuf("bias_b", [1, 6 * D], BF16)
    grow = p.sbuf("grow", [1, 3 * D], BF16)
    brow = p.sbuf("brow", [1, 3 * D], BF16)
    gb = p.sbuf("gb", [128, 3 * D], F32)
    bb = p.sbuf("bb", [128, 3 * D], F32)
    h = p.sbuf("h", [128, D], F32)
    hb = p.sbuf("hb", [128, D], BF16)
    hT = p.sbuf("hT", [128, D], BF16)
    ub = p.sbuf("ub", [128, 32], BF16)
    uT = p.sbuf("uT", [32, 128], BF16)
    y = p.sbuf("y", [128, D], F32)
    tsb = p.sbuf("tsb", [128, D], F32)
    scr = p.sbuf("scr", [128, D], F32)
    st1 = p.sbuf("st1", [128, 2], F32)
    st2 = p.sbuf("st2", [128, 2], F32)
    pA = p.psum("pA", [128, 512], F32)
    pU = p.psum("pU", [128, 128], F32)
    pT = p.psum("pT", [128, 512], BF16)

    for m in range(6):
        for k in range(8):
            r0 = m * D + k * 128
            p.dma(qi[:, :], qwT[r0:r0 + 128, :])
            p.dma(sc[:, :], screp[r0:r0 + 128, :])
            p.emit("vector", lambda e: e.tensor_copy(qf[:, :], qi[:, :]))
            p.emit("vector", lambda e: e.scalar_tensor_tensor(
                qf[:, :], qf[:, :], 1.0 / 7.5, sc[:, :], op0=OP.mult, op1=OP.mult))
            p.emit("vector", lambda e, m=m, k=k: e.tensor_sub(
                wbs[m][:, k * D:(k + 1) * D], qf[:, :], sc[:, :]))
    for m in range(6):
        for k in range(8):
            p.dma(stag[:, :32], laT[m * D + k * 128: m * D + (k + 1) * 128, :])
            p.emit("vector", lambda e, m=m, k=k: e.tensor_copy(
                lab[:, (m * 8 + k) * 32:(m * 8 + k + 1) * 32], stag[:, :32]))
    for m in range(6):
        p.dma(stag[:32, :D], lbT[m * 32:(m + 1) * 32, :])
        p.emit("vector", lambda e, m=m: e.tensor_copy(lbb[:, m * D:(m + 1) * D],
                                                      stag[:32, :D]))
    for m in range(6):
        p.dma(srow[0:1, :D], biases[m:m + 1, :])
        p.emit("vector", lambda e, m=m: e.tensor_copy(bias_b[0:1, m * D:(m + 1) * D],
                                                      srow[0:1, :D]))
    for i in range(3):
        p.dma(srow[0:1, :D], lng[i:i + 1, :])
        p.emit("vector", lambda e, i=i: e.tensor_copy(grow[0:1, i * D:(i + 1) * D],
                                                      srow[0:1, :D]))
        p.dma(srow[0:1, :D], lnb[i:i + 1, :])
        p.emit("vector", lambda e, i=i: e.tensor_copy(brow[0:1, i * D:(i + 1) * D],
                                                      srow[0:1, :D]))
    bcast_row(p, ones1, grow, pA, gb, 3 * D)
    bcast_row(p, ones1, brow, pA, bb, 3 * D)

    def qlora(src_bf, srcT, m, out_sb, resid=None):
        lhs_u = [srcT[:, k * 128:(k + 1) * 128] for k in range(8)]
        rhs_u = [lab[:, (m * 8 + k) * 32:(m * 8 + k + 1) * 32] for k in range(8)]
        mm_acc(p, pU[:, 0:32], lhs_u, rhs_u)
        p.emit("vector", lambda e: e.tensor_copy(ub[:, :], pU[:, 0:32]))
        p.emit("tensor", lambda e: e.transpose(pT[0:32, :128], ub[:, :], idb[:, :]))
        p.emit("vector", lambda e: e.tensor_copy(uT[:, :], pT[0:32, :128]))
        for c0 in range(0, D, 512):
            fns = []
            fns.append(lambda e, c0=c0: e.matmul(pA[:, :512], ones1[0:1, :],
                                                 bias_b[0:1, m * D + c0: m * D + c0 + 512],
                                                 start=True, stop=False))
            for k in range(8):
                fns.append(lambda e, k=k, c0=c0: e.matmul(
                    pA[:, :512], srcT[:, k * 128:(k + 1) * 128],
                    wbs[m][:, k * D + c0: k * D + c0 + 512], start=False, stop=False))
            fns.append(lambda e, c0=c0: e.matmul(
                pA[:, :512], uT[:, :], lbb[:, m * D + c0: m * D + c0 + 512],
                start=False, stop=True))
            p.group("tensor", fns)
            if resid is None:
                p.emit("vector", lambda e, c0=c0: e.tensor_copy(out_sb[:, c0:c0 + 512],
                                                                pA[:, :512]))
            else:
                p.emit("vector", lambda e, c0=c0: e.tensor_add(
                    out_sb[:, c0:c0 + 512], pA[:, :512], resid[:, c0:c0 + 512]))

    xin = p.sbuf("xin", [128, D], F32)
    for t in range(TIL):
        p.dma(xin[:, :], x[t * 128:(t + 1) * 128, :])
        ln_stats(p, xin, st1, st2, scr, D)
        p.emit("vector", lambda e: e.tensor_scalar(h[:, :], xin[:, :], st1[:, 0:1],
                                                   st2[:, 0:1], op0=OP.add, op1=OP.mult))
        for blk in range(3):
            p.emit("vector", lambda e: e.tensor_copy(hb[:, :], h[:, :]))
            tr_in(p, hb, hT, pT, idb, 8)
            qlora(hb, hT, 2 * blk, y)
            ln_stats(p, y, st1, st2, scr, D)
            p.emit("vector", lambda e: e.tensor_scalar(tsb[:, :], y[:, :], st1[:, 0:1],
                                                       st2[:, 0:1], op0=OP.add, op1=OP.mult))
            p.emit("vector", lambda e, blk=blk: e.tensor_mul(
                tsb[:, :], tsb[:, :], gb[:, blk * D:(blk + 1) * D]))
            p.emit("vector", lambda e, blk=blk: e.tensor_add(
                tsb[:, :], tsb[:, :], bb[:, blk * D:(blk + 1) * D]))
            p.emit("scalar", lambda e: e.activation(hb[:, :], tsb[:, :], AF.Gelu))
            tr_in(p, hb, hT, pT, idb, 8)
            qlora(hb, hT, 2 * blk + 1, h, resid=h)
        p.emit("vector", lambda e: e.tensor_add(scr[:, :], h[:, :], xin[:, :]))
        p.dma(o_x2[t * 128:(t + 1) * 128, :], scr[:, :])
    return p.build()


def build_P30():
    p = P()
    a = p.inp("a", [NT, D], F32)
    bb_ = p.inp("b", [NT, D], F32)
    o_s = p.outp("s", [NT, D], F32)
    o_ln = p.outp("ln", [NT, D], F32)
    at = p.sbuf("at", [128, D], F32)
    bt = p.sbuf("bt", [128, D], F32)
    scr = p.sbuf("scr", [128, D], F32)
    st1 = p.sbuf("st1", [128, 2], F32)
    st2 = p.sbuf("st2", [128, 2], F32)
    for t in range(TIL):
        p.dma(at[:, :], a[t * 128:(t + 1) * 128, :])
        p.dma(bt[:, :], bb_[t * 128:(t + 1) * 128, :])
        p.emit("vector", lambda e: e.tensor_add(at[:, :], at[:, :], bt[:, :]))
        p.dma(o_s[t * 128:(t + 1) * 128, :], at[:, :])
        ln_stats(p, at, st1, st2, scr, D)
        p.emit("vector", lambda e: e.tensor_scalar(bt[:, :], at[:, :], st1[:, 0:1],
                                                   st2[:, 0:1], op0=OP.add, op1=OP.mult))
        p.dma(o_ln[t * 128:(t + 1) * 128, :], bt[:, :])
    return p.build()


def build_S4():
    p = P()
    ln = p.inp("ln", [NT, D], F32)
    wqT = p.inp("wqT", [D, D], F32)
    bq = p.inp("bq", [1, D], F32)
    wvT = p.inp("wvT", [D, D], F32)
    bv = p.inp("bv", [1, D], F32)
    hyp = p.inp("hyp", [65, 6], F32)
    idf_in = p.inp("identf", [128, 128], F32)
    o_qT = p.outp("qT", [D, NT], BF16)
    o_v = p.outp("v", [16, NT, 64], BF16)
    o_bkt = p.outp("bkt", [NT, 16], F32)

    cm = common_alloc(p)
    idb, ones1 = cm["idb"], cm["ones1"]
    idf = p.sbuf("idf", [128, 128], F32)
    p.dma(idf[:, :], idf_in[:, :])
    stag = p.sbuf("stag", [128, D], F32)
    wq = p.sbuf("wq", [128, 8 * D], BF16)
    wv = p.sbuf("wv", [128, 8 * D], BF16)
    srow = p.sbuf("srow", [1, D], F32)
    bqb = p.sbuf("bqb", [1, D], BF16)
    bvb = p.sbuf("bvb", [1, D], BF16)
    hps = p.sbuf("hps", [128, 6], F32)
    hpr = p.sbuf("hpr", [1, 6], F32)
    pw2 = p.sbuf("pw2", [128, 6], F32)
    ones1f = p.sbuf("ones1f", [1, 128], F32)
    xt = p.sbuf("xt", [128, D], F32)
    xnb = p.sbuf("xnb", [128, D], BF16)
    xnT = p.sbuf("xnT", [128, D], BF16)
    qf = p.sbuf("qf", [128, D], F32)
    qb = p.sbuf("qb", [128, D], BF16)
    vb = p.sbuf("vb", [128, D], BF16)
    qTf = p.sbuf("qTf", [128, D], F32)
    qTsb = p.sbuf("qTsb", [128, 8 * NT], BF16)
    bit = p.sbuf("bit", [128, 6], F32)
    bcol = p.sbuf("bcol", [128, 16], F32)
    pA = p.psum("pA", [128, 512], F32)
    pD = p.psum("pD", [128, 128], F32)
    pT = p.psum("pT", [128, 512], BF16)
    pTF = p.psum("pTF", [128, 512], F32)

    load_cast_w(p, wqT, stag, wq, 8, D)
    load_cast_w(p, wvT, stag, wv, 8, D)
    load_row_bf(p, bq[0:1, :], srow, bqb, D)
    load_row_bf(p, bv[0:1, :], srow, bvb, D)
    p.dma(hps[0:64, :], hyp[0:64, :])
    p.dma(hps[64:128, :], hyp[0:64, :])
    p.dma(hpr[0:1, :], hyp[64:65, :])
    for n in range(6):
        p.emit("vector", lambda e, n=n: e.memset(pw2[:, n:n + 1], float(2 ** n)))
    p.emit("vector", lambda e: e.memset(ones1f[:, :], 1.0))

    for t in range(TIL):
        p.dma(xt[:, :], ln[t * 128:(t + 1) * 128, :])
        p.emit("vector", lambda e: e.tensor_copy(xnb[:, :], xt[:, :]))
        tr_in(p, xnb, xnT, pT, idb, 8)
        lhs = [xnT[:, k * 128:(k + 1) * 128] for k in range(8)]
        for c0 in range(0, D, 512):
            rhs = [wq[:, k * D + c0: k * D + c0 + 512] for k in range(8)]
            mm_acc(p, pA[:, :512], lhs, rhs, bias=(ones1[0:1, :], bqb[0:1, c0:c0 + 512]))
            p.emit("vector", lambda e, c0=c0: e.tensor_copy(qf[:, c0:c0 + 512], pA[:, :512]))
        p.emit("vector", lambda e: e.tensor_copy(qb[:, :], qf[:, :]))
        for c0 in range(0, D, 512):
            rhs = [wv[:, k * D + c0: k * D + c0 + 512] for k in range(8)]
            mm_acc(p, pA[:, :512], lhs, rhs, bias=(ones1[0:1, :], bvb[0:1, c0:c0 + 512]))
            p.emit("vector", lambda e, c0=c0: e.tensor_copy(vb[:, c0:c0 + 512], pA[:, :512]))
        for h in range(16):
            p.dma(o_v[h, t * 128:(t + 1) * 128, :], vb[:, h * 64:(h + 1) * 64])
        for k in range(8):
            p.emit("tensor", lambda e, k=k: e.transpose(pT[:, :128],
                                                        qb[:, k * 128:(k + 1) * 128], idb[:, :]))
            p.emit("vector", lambda e, k=k, t=t: e.tensor_copy(
                qTsb[:, k * NT + t * 128: k * NT + (t + 1) * 128], pT[:, :128]))
        for k in range(8):
            p.emit("tensor", lambda e, k=k: e.transpose(pTF[:, :128],
                                                        qf[:, k * 128:(k + 1) * 128], idf[:, :]))
            p.emit("vector", lambda e, k=k: e.tensor_copy(
                qTf[:, k * 128:(k + 1) * 128], pTF[:, :128]))
        for h in range(16):
            par, pr = h % 2, h // 2
            p.group("tensor", [
                lambda e: e.matmul(pD[:, 0:6], ones1f[0:1, :], hpr[0:1, :], start=True, stop=False),
                lambda e, par=par, pr=pr: e.matmul(
                    pD[:, 0:6], qTf[par * 64:(par + 1) * 64, pr * 128:(pr + 1) * 128],
                    hps[par * 64:(par + 1) * 64, :], start=False, stop=True),
            ])
            p.emit("vector", lambda e: e.tensor_single_scalar(bit[:, :], pD[:, 0:6], 0.0,
                                                              op=OP.is_ge))
            p.emit("vector", lambda e: e.tensor_mul(bit[:, :], bit[:, :], pw2[:, :]))
            p.emit("vector", lambda e, h=h: e.reduce_sum(bcol[:, h:h + 1], bit[:, :],
                                                         axis=AX))
        p.dma(o_bkt[t * 128:(t + 1) * 128, :], bcol[:, :])
    for k in range(8):
        p.dma(o_qT[k * 128:(k + 1) * 128, :], qTsb[:, k * NT:(k + 1) * NT])
    return p.build()


def build_S5():
    p = P()
    S = T
    qTh = p.inp("qTh", [64, S], BF16)
    vh = p.inp("vh", [S, 64], BF16)
    bkh = p.inp("bkh", [S, 1], F32)
    o_ao = p.outp("ao", [S, 64], F32)

    cm = common_alloc(p)
    idb, ones1 = cm["idb"], cm["ones1"]
    qsb = p.sbuf("qsb", [64, S], BF16)
    vsb = p.sbuf("vsb", [128, 64 * 64], BF16)
    brow = p.sbuf("brow", [1, S], F32)
    browb = p.sbuf("browb", [1, S], BF16)
    bkbc = p.sbuf("bkbc", [128, S], BF16)
    bqc = p.sbuf("bqc", [128, 1], F32)
    cf = p.sbuf("cf", [128, 512], BF16)
    s_sb = p.sbuf("s_sb", [128, S], F32)
    e_sb = p.sbuf("e_sb", [128, S], F32)
    attb = p.sbuf("attb", [128, S], BF16)
    atT = p.sbuf("atT", [128, 512], BF16)
    m1 = p.sbuf("m1", [128, 1], F32)
    se = p.sbuf("se", [128, 1], F32)
    osb = p.sbuf("osb", [128, 64], F32)
    pF = p.psum("pF", [128, 512], F32)
    pG = p.psum("pG", [128, 64], F32)
    pT = p.psum("pT", [128, 512], BF16)
    pE = p.psum("pE", [128, 512], F32)

    p.dma(qsb[:, :], qTh[:, :])
    for g in range(8):
        p.dma(vsb.rearrange("p (c d) -> p c d", d=64)[:, g * 8:(g + 1) * 8, :],
              vh.rearrange("(c p) d -> p c d", p=128)[:, g * 8:(g + 1) * 8, :])
    p.dma(brow[:, :], bkh.rearrange("a b -> b a"))
    p.emit("vector", lambda e: e.tensor_copy(browb[:, :], brow[:, :]))
    for j in range(16):
        p.group("tensor", [lambda e, j=j: e.matmul(
            pE[:, :512], ones1[0:1, :], browb[0:1, j * 512:(j + 1) * 512],
            start=True, stop=True)])
        p.emit("vector", lambda e, j=j: e.tensor_copy(bkbc[:, j * 512:(j + 1) * 512],
                                                      pE[:, :512]))
    for i in range(64):
        p.dma(bqc[:, 0:1], bkh[i * 128:(i + 1) * 128, 0:1])
        for kc in range(16):
            p.group("tensor", [lambda e, i=i, kc=kc: e.matmul(
                pF[:, :512], qsb[:, i * 128:(i + 1) * 128],
                qsb[:, kc * 512:(kc + 1) * 512], start=True, stop=True)])
            p.emit("vector", lambda e, kc=kc: e.tensor_scalar(
                cf[:, :], bkbc[:, kc * 512:(kc + 1) * 512], bqc[:, 0:1], 62.0,
                op0=OP.is_equal, op1=OP.add))
            p.emit("vector", lambda e, kc=kc: e.scalar_tensor_tensor(
                s_sb[:, kc * 512:(kc + 1) * 512], pF[:, :512], 1.0 / 32.0, cf[:, :],
                op0=OP.mult, op1=OP.mult))
        p.emit("vector", lambda e: e.reduce_max(m1[:, :], s_sb[:, :], axis=AX))
        p.emit("vector", lambda e: e.tensor_scalar_mul(m1[:, :], m1[:, :], -1.0))
        p.emit("scalar", lambda e: e.activation(e_sb[:, :], s_sb[:, :], AF.Exp,
                                                bias=m1[:, 0:1], accum_out=se[:, 0:1]))
        p.emit("vector", lambda e: e.reciprocal(se[:, :], se[:, :]))
        p.emit("vector", lambda e: e.tensor_scalar_mul(attb[:, :], e_sb[:, :], se[:, 0:1]))
        for g in range(16):
            for j in range(4):
                p.emit("tensor", lambda e, g=g, j=j: e.transpose(
                    pT[:, j * 128:(j + 1) * 128],
                    attb[:, (4 * g + j) * 128:(4 * g + j + 1) * 128], idb[:, :]))
            p.emit("vector", lambda e: e.tensor_copy(atT[:, :], pT[:, :512]))
            p.group("tensor", [lambda e, g=g, j=j: e.matmul(
                pG[:, 0:64], atT[:, j * 128:(j + 1) * 128],
                vsb[:, (4 * g + j) * 64:(4 * g + j + 1) * 64],
                start=(g == 0 and j == 0), stop=(g == 15 and j == 3))
                for j in range(4)])
        p.emit("vector", lambda e: e.tensor_copy(osb[:, :], pG[:, 0:64]))
        p.dma(o_ao[i * 128:(i + 1) * 128, :], osb[:, :])
    return p.build()


def build_S7():
    p = P()
    HMAX = 18
    hh = p.inp("hh", [2305, D], F32)
    wqkv = p.inp("wqkv", [D, 3072], F32)
    wo = p.inp("wo", [D, D], F32)
    qsc = p.inp("qsc", [1, 64], F32)
    ksc = p.inp("ksc", [1, 64], F32)
    tabs = p.inp("tabs", [4, 2304, 64], F32)
    masks = p.inp("masks", [TIL, 128, 512], F32)
    o_h = p.outp("h", [NT, D], F32)

    cm = common_alloc(p)
    idb, ones1 = cm["idb"], cm["ones1"]
    stag = p.sbuf("stag", [128, D], F32)
    wq = p.sbuf("wq", [128, 8 * 3072], BF16)
    wos = p.sbuf("wos", [128, 8 * D], BF16)
    srow = p.sbuf("srow", [1, 64], F32)
    qscb = p.sbuf("qscb", [1, 64], BF16)
    kscb = p.sbuf("kscb", [1, 64], BF16)
    qsb_ = p.sbuf("qsb_", [128, 64], F32)
    ksb_ = p.sbuf("ksb_", [128, 64], F32)
    xsh = p.sbuf("xsh", [128, D], F32)
    xb = p.sbuf("xb", [128, D], BF16)
    xT = p.sbuf("xT", [128, D], BF16)
    qf = p.sbuf("qf", [128, D], F32)
    kf = p.sbuf("kf", [128, D], F32)
    sq = p.sbuf("sq", [128, D], F32)
    ssq = p.sbuf("ssq", [128, 16], F32)
    rh = p.sbuf("rh", [128, D], F32)
    t1_ = sq
    tc = p.sbuf("tc", [128, 64], F32)
    ts = p.sbuf("ts", [128, 64], F32)
    qT_all = p.sbuf("qT_all", [128, TIL * D], BF16)
    kT_all = p.sbuf("kT_all", [128, HMAX * D], BF16)
    vb_all = p.sbuf("vb_all", [128, HMAX * D], BF16)
    msk = p.sbuf("msk", [128, 512], F32)
    s_sb = p.sbuf("s_sb", [128, 512], F32)
    e_sb2 = p.sbuf("e_sb2", [128, 512], F32)
    attb = p.sbuf("attb", [128, 512], BF16)
    atT = p.sbuf("atT", [128, 512], BF16)
    m1 = p.sbuf("m1", [128, 1], F32)
    se = p.sbuf("se", [128, 1], F32)
    ao = qf
    aob = xb
    aoT = xT
    res = rh
    pA = p.psum("pA", [128, 512], F32)
    pF = p.psum("pF", [128, 512], F32)
    pG = p.psum("pG", [128, 64], F32)
    pT = p.psum("pT", [128, 512], BF16)

    for k in range(8):
        for s4 in range(3):
            p.dma(stag[:, :D], wqkv[k * 128:(k + 1) * 128, s4 * D:(s4 + 1) * D])
            p.emit("vector", lambda e, k=k, s4=s4: e.tensor_copy(
                wq[:, k * 3072 + s4 * D: k * 3072 + (s4 + 1) * D], stag[:, :D]))
    load_cast_w(p, wo, stag, wos, 8, D)
    load_row_bf(p, qsc[0:1, :], srow, qscb, 64)
    load_row_bf(p, ksc[0:1, :], srow, kscb, 64)
    bcast_row(p, ones1, qscb, pA, qsb_, 64)
    bcast_row(p, ones1, kscb, pA, ksb_, 64)

    def norm_rot(dst_f32, scale_b, ct, st_):
        d3 = dst_f32.rearrange("p (a b) -> p a b", b=64)
        sq3 = sq.rearrange("p (a b) -> p a b", b=64)
        rh3 = rh.rearrange("p (a b) -> p a b", b=64)
        t13 = t1_.rearrange("p (a b) -> p a b", b=64)
        p.emit("scalar", lambda e: e.activation(sq[:, :], dst_f32[:, :], AF.Square))
        p.emit("vector", lambda e: e.reduce_sum(ssq[:, :], sq3[:, :, :], axis=AX))
        p.emit("scalar", lambda e: e.sqrt(ssq[:, :], ssq[:, :]))
        p.emit("vector", lambda e: e.tensor_scalar_max(ssq[:, :], ssq[:, :], 1e-12))
        p.emit("vector", lambda e: e.reciprocal(ssq[:, :], ssq[:, :]))
        ssq3 = ssq.rearrange("p (a b) -> p a b", b=1)
        p.emit("vector", lambda e: e.tensor_tensor(
            d3[:, :, :], d3[:, :, :], ssq3[:, :, 0:1].broadcast_to([128, 16, 64]),
            op=OP.mult))
        sc3 = scale_b.rearrange("p (a b) -> p a b", a=1)
        p.emit("vector", lambda e: e.tensor_tensor(
            d3[:, :, :], d3[:, :, :], sc3[:, 0:1, :].broadcast_to([128, 16, 64]),
            op=OP.mult))
        p.emit("scalar", lambda e: e.mul(rh3[:, :, 0:32], d3[:, :, 32:64], -1.0))
        p.emit("vector", lambda e: e.tensor_copy(rh3[:, :, 32:64], d3[:, :, 0:32]))
        ct3 = ct.rearrange("p (a b) -> p a b", a=1)
        st3 = st_.rearrange("p (a b) -> p a b", a=1)
        p.emit("vector", lambda e: e.tensor_tensor(
            t13[:, :, :], d3[:, :, :], ct3[:, 0:1, :].broadcast_to([128, 16, 64]),
            op=OP.mult))
        p.emit("vector", lambda e: e.tensor_tensor(
            rh3[:, :, :], rh3[:, :, :], st3[:, 0:1, :].broadcast_to([128, 16, 64]),
            op=OP.mult))
        p.emit("vector", lambda e: e.tensor_add(dst_f32[:, :], t1_[:, :], rh[:, :]))

    for kt in range(HMAX):
        p.dma(xsh[:, 0:512], hh[1 + kt * 128: 1 + (kt + 1) * 128, 0:512])
        p.dma(xsh[:, 512:D], hh[kt * 128: (kt + 1) * 128, 512:D])
        p.emit("vector", lambda e: e.tensor_copy(xb[:, :], xsh[:, :]))
        tr_in(p, xb, xT, pT, idb, 8)
        lhs = [xT[:, k * 128:(k + 1) * 128] for k in range(8)]
        for c0 in range(0, 3072, 512):
            rhs = [wq[:, k * 3072 + c0: k * 3072 + c0 + 512] for k in range(8)]
            mm_acc(p, pA[:, :512], lhs, rhs)
            if c0 < D:
                p.emit("vector", lambda e, c0=c0: e.tensor_copy(qf[:, c0:c0 + 512],
                                                                pA[:, :512]))
            elif c0 < 2 * D:
                p.emit("vector", lambda e, c0=c0: e.tensor_copy(kf[:, c0 - D:c0 - D + 512],
                                                                pA[:, :512]))
            else:
                p.emit("vector", lambda e, c0=c0, kt=kt: e.tensor_copy(
                    vb_all[:, kt * D + c0 - 2 * D: kt * D + c0 - 2 * D + 512], pA[:, :512]))
        p.dma(tc[:, :], tabs[2, kt * 128:(kt + 1) * 128, :])
        p.dma(ts[:, :], tabs[3, kt * 128:(kt + 1) * 128, :])
        norm_rot(kf, ksb_, tc, ts)
        p.emit("vector", lambda e: e.tensor_copy(xb[:, :], kf[:, :]))
        tr_in(p, xb, kT_all, pT, idb, 8, dst_col0=kt * D)
        if kt >= 2:
            i = kt - 2
            p.dma(tc[:, :], tabs[0, kt * 128:(kt + 1) * 128, :])
            p.dma(ts[:, :], tabs[1, kt * 128:(kt + 1) * 128, :])
            norm_rot(qf, qsb_, tc, ts)
            p.emit("vector", lambda e: e.tensor_copy(xb[:, :], qf[:, :]))
            tr_in(p, xb, qT_all, pT, idb, 8, dst_col0=i * D)
    for i in range(TIL):
        ht0 = (i // 2) * 2
        p.dma(msk[:, :], masks[i, :, :])
        for h in range(16):
            par, pr = h % 2, h // 2
            lq = qT_all[par * 64:(par + 1) * 64, i * D + pr * 128: i * D + (pr + 1) * 128]
            p.group("tensor", [lambda e, j=j, lq=lq, par=par, pr=pr, ht0=ht0: e.matmul(
                pF[:, j * 128:(j + 1) * 128], lq,
                kT_all[par * 64:(par + 1) * 64,
                       (ht0 + j) * D + pr * 128: (ht0 + j) * D + (pr + 1) * 128],
                start=True, stop=True) for j in range(4)])
            p.emit("vector", lambda e: e.scalar_tensor_tensor(
                s_sb[:, :], pF[:, :512], 8.0, msk[:, :], op0=OP.mult, op1=OP.add))
            p.emit("vector", lambda e: e.reduce_max(m1[:, :], s_sb[:, :], axis=AX))
            p.emit("vector", lambda e: e.tensor_scalar_mul(m1[:, :], m1[:, :], -1.0))
            p.emit("scalar", lambda e: e.activation(e_sb2[:, :], s_sb[:, :], AF.Exp,
                                                    bias=m1[:, 0:1], accum_out=se[:, 0:1]))
            p.emit("vector", lambda e: e.reciprocal(se[:, :], se[:, :]))
            p.emit("vector", lambda e: e.tensor_scalar_mul(attb[:, :], e_sb2[:, :],
                                                           se[:, 0:1]))
            for j in range(4):
                p.emit("tensor", lambda e, j=j: e.transpose(
                    pT[:, j * 128:(j + 1) * 128], attb[:, j * 128:(j + 1) * 128], idb[:, :]))
            p.emit("vector", lambda e: e.tensor_copy(atT[:, :], pT[:, :512]))
            p.group("tensor", [lambda e, j=j, h=h, ht0=ht0: e.matmul(
                pG[:, 0:64], atT[:, j * 128:(j + 1) * 128],
                vb_all[:, (ht0 + j) * D + h * 64: (ht0 + j) * D + (h + 1) * 64],
                start=(j == 0), stop=(j == 3)) for j in range(4)])
            p.emit("vector", lambda e, h=h: e.tensor_copy(ao[:, h * 64:(h + 1) * 64],
                                                          pG[:, 0:64]))
        p.emit("vector", lambda e: e.tensor_copy(aob[:, :], ao[:, :]))
        tr_in(p, aob, aoT, pT, idb, 8)
        p.dma(xsh[:, :], hh[257 + i * 128: 257 + (i + 1) * 128, :])
        lhs2 = [aoT[:, k * 128:(k + 1) * 128] for k in range(8)]
        for c0 in range(0, D, 512):
            rhs = [wos[:, k * D + c0: k * D + c0 + 512] for k in range(8)]
            mm_acc(p, pA[:, :512], lhs2, rhs)
            p.emit("vector", lambda e, c0=c0: e.tensor_add(res[:, c0:c0 + 512],
                                                           pA[:, :512], xsh[:, c0:c0 + 512]))
        p.dma(o_h[i * 128:(i + 1) * 128, :], res[:, :])
    return p.build()


DIN = 2730


def build_S8():
    p = P()
    hh = p.inp("hh", [NT + 1, D], F32)
    lng = p.inp("lng", [1, D], F32)
    lnb = p.inp("lnb", [1, D], F32)
    w1 = p.inp("w1", [D, 2 * DIN], F32)
    b1 = p.inp("b1", [1, 2 * DIN], F32)
    w2 = p.inp("w2", [DIN, D], F32)
    b2 = p.inp("b2", [1, D], F32)
    o_h = p.outp("h", [NT, D], F32)

    cm = common_alloc(p)
    idb, ones1 = cm["idb"], cm["ones1"]
    stag = p.sbuf("stag", [128, 1024], F32)
    w1b = p.sbuf("w1b", [128, 8 * 2 * DIN], BF16)
    w2b = p.sbuf("w2b", [128, 22 * D], BF16)
    srow = p.sbuf("srow", [1, 2 * DIN], F32)
    b1b = p.sbuf("b1b", [1, 2 * DIN], BF16)
    b2b = p.sbuf("b2b", [1, D], BF16)
    grow = p.sbuf("grow", [1, D], BF16)
    brow = p.sbuf("brow", [1, D], BF16)
    gb = p.sbuf("gb", [128, D], F32)
    bb2 = p.sbuf("bb2", [128, D], F32)
    xsh = p.sbuf("xsh", [128, D], F32)
    scr = p.sbuf("scr", [128, D], F32)
    st1 = p.sbuf("st1", [128, 2], F32)
    st2 = p.sbuf("st2", [128, 2], F32)
    xnb = p.sbuf("xnb", [128, D], BF16)
    xT = p.sbuf("xT", [128, D], BF16)
    gg = p.sbuf("gg", [128, 512], F32)
    pb = p.sbuf("pb", [128, 512], BF16)
    pbT = p.sbuf("pbT", [128, 512], BF16)
    res = p.sbuf("res", [128, D], F32)
    pA = p.psum("pA", [128, 512], F32)
    pB = p.psum("pB", [128, 512], F32)
    pC = p.psum("pC", [128, 512], F32)
    pD = p.psum("pD", [128, 512], F32)
    pT = p.psum("pT", [128, 512], BF16)

    for k in range(8):
        for s4 in range(6):
            w = min(1024, 2 * DIN - s4 * 1024)
            p.dma(stag[:, :w], w1[k * 128:(k + 1) * 128, s4 * 1024:s4 * 1024 + w])
            p.emit("vector", lambda e, k=k, s4=s4, w=w: e.tensor_copy(
                w1b[:, k * 2 * DIN + s4 * 1024: k * 2 * DIN + s4 * 1024 + w], stag[:, :w]))
    for k in range(22):
        r = min(128, DIN - k * 128)
        p.dma(stag[:r, :D], w2[k * 128:k * 128 + r, :])
        p.emit("vector", lambda e, k=k, r=r: e.tensor_copy(
            w2b[:r, k * D:(k + 1) * D], stag[:r, :D]))
    load_row_bf(p, b1[0:1, :], srow, b1b, 2 * DIN)
    load_row_bf(p, b2[0:1, :], srow, b2b, D)
    load_row_bf(p, lng[0:1, :], srow, grow, D)
    load_row_bf(p, lnb[0:1, :], srow, brow, D)
    bcast_row(p, ones1, grow, pA, gb, D)
    bcast_row(p, ones1, brow, pA, bb2, D)

    CH = [(0, 512), (512, 512), (1024, 512), (1536, 512), (2048, 512), (2560, 170)]
    for t in range(TIL):
        p.dma(xsh[:, 0:512], hh[1 + t * 128: 1 + (t + 1) * 128, 0:512])
        p.dma(xsh[:, 512:D], hh[t * 128: (t + 1) * 128, 512:D])
        ln_stats(p, xsh, st1, st2, scr, D)
        p.emit("vector", lambda e: e.tensor_scalar(scr[:, :], xsh[:, :], st1[:, 0:1],
                                                   st2[:, 0:1], op0=OP.add, op1=OP.mult))
        p.emit("vector", lambda e: e.tensor_mul(scr[:, :], scr[:, :], gb[:, :]))
        p.emit("vector", lambda e: e.tensor_add(scr[:, :], scr[:, :], bb2[:, :]))
        p.emit("vector", lambda e: e.tensor_copy(xnb[:, :], scr[:, :]))
        tr_in(p, xnb, xT, pT, idb, 8)
        lhs = [xT[:, k * 128:(k + 1) * 128] for k in range(8)]
        first = True
        for (c0, w) in CH:
            rhs_a = [w1b[:, k * 2 * DIN + c0: k * 2 * DIN + c0 + w] for k in range(8)]
            rhs_g = [w1b[:, k * 2 * DIN + DIN + c0: k * 2 * DIN + DIN + c0 + w]
                     for k in range(8)]
            mm_acc(p, pA[:, :w], lhs, rhs_a, bias=(ones1[0:1, :], b1b[0:1, c0:c0 + w]))
            mm_acc(p, pB[:, :w], lhs, rhs_g,
                   bias=(ones1[0:1, :], b1b[0:1, DIN + c0:DIN + c0 + w]))
            p.emit("scalar", lambda e, w=w: e.activation(gg[:, :w], pB[:, :w], AF.Gelu))
            p.emit("vector", lambda e, w=w: e.tensor_mul(pb[:, :w], pA[:, :w], gg[:, :w]))
            subs = [(j * 128, min(128, w - j * 128)) for j in range((w + 127) // 128)]
            for si, (j0, jw) in enumerate(subs):
                p.emit("tensor", lambda e, j0=j0, jw=jw, si=si: e.transpose(
                    pT[0:jw, si * 128:(si + 1) * 128], pb[:, j0:j0 + jw], idb[:, :]))
            p.emit("vector", lambda e, ns=len(subs): e.tensor_copy(
                pbT[:, :ns * 128], pT[:, :ns * 128]))
            last_chunk = (c0 + w == 2 * DIN) or (c0 + w == DIN)
            fnsC, fnsD = [], []
            for si, (j0, jw) in enumerate(subs):
                kk = (c0 + j0) // 128
                lastk = last_chunk and (si == len(subs) - 1)
                for (pp, fns, off) in ((pC, fnsC, 0), (pD, fnsD, 512)):
                    fns.append(lambda e, si=si, jw=jw, kk=kk, pp=pp, off=off, lastk=lastk:
                               e.matmul(pp[:, :512],
                                        pbT[0:jw, si * 128: si * 128 + 128],
                                        w2b[0:jw, kk * D + off: kk * D + off + 512],
                                        start=False, stop=lastk))
            if first:
                fnsC.insert(0, lambda e: e.matmul(pC[:, :512], ones1[0:1, :],
                                                  b2b[0:1, 0:512], start=True, stop=False))
                fnsD.insert(0, lambda e: e.matmul(pD[:, :512], ones1[0:1, :],
                                                  b2b[0:1, 512:1024], start=True, stop=False))
                first = False
            p.group("tensor", fnsC)
            p.group("tensor", fnsD)
        p.dma(xsh[:, :], hh[1 + t * 128: 1 + (t + 1) * 128, :])
        p.emit("vector", lambda e: e.tensor_add(res[:, 0:512], pC[:, :512], xsh[:, 0:512]))
        p.emit("vector", lambda e: e.tensor_add(res[:, 512:D], pD[:, :512], xsh[:, 512:D]))
        p.dma(o_h[t * 128:(t + 1) * 128, :], res[:, :])
    return p.build()


def build_S9():
    p = P()
    ln = p.inp("ln", [NT, D], F32)
    wq = p.inp("wq", [D, D], F32)
    wk = p.inp("wk", [D, D], F32)
    wv = p.inp("wv", [D, D], F32)
    projT = p.inp("projT", [64, 64], F32)
    idf_in = p.inp("identf", [128, 128], F32)
    o_qd = p.outp("qd", [NT, D], F32)
    o_kd = p.outp("kd", [NT, D], F32)
    o_diag = p.outp("diag", [NT, 32], F32)
    o_v = p.outp("v", [NT, D], BF16)
    o_kmax = p.outp("kmax", [1, 1], F32)

    cm = common_alloc(p)
    idb, ones1 = cm["idb"], cm["ones1"]
    idf = p.sbuf("idf", [128, 128], F32)
    p.dma(idf[:, :], idf_in[:, :])
    stag = p.sbuf("stag", [128, D], F32)
    wqb = p.sbuf("wqb", [128, 8 * D], BF16)
    wkb = p.sbuf("wkb", [128, 8 * D], BF16)
    wvb = p.sbuf("wvb", [128, 8 * D], BF16)
    pjs = p.sbuf("pjs", [128, 64], F32)
    pjb = p.sbuf("pjb", [128, 64], BF16)
    xt = p.sbuf("xt", [128, D], F32)
    xnb = p.sbuf("xnb", [128, D], BF16)
    xnT = p.sbuf("xnT", [128, D], BF16)
    qf = p.sbuf("qf", [128, D], F32)
    qb = p.sbuf("qb", [128, D], BF16)
    qT = p.sbuf("qT", [128, D], BF16)
    sq = p.sbuf("sq", [128, D], F32)
    dg = p.sbuf("dg", [128, 32], F32)
    feat = p.sbuf("feat", [128, D], F32)
    kmc = p.sbuf("kmc", [128, 1], F32)
    rmx = p.sbuf("rmx", [128, 1], F32)
    krow = p.sbuf("krow", [1, 128], F32)
    kmx = p.sbuf("kmx", [1, 1], F32)
    pA = p.psum("pA", [128, 512], F32)
    pH = p.psum("pH", [128, 64], F32)
    pT = p.psum("pT", [128, 512], BF16)
    pTF = p.psum("pTF", [128, 128], F32)

    DN = float(64.0 ** -0.25)
    load_cast_w(p, wq, stag, wqb, 8, D)
    load_cast_w(p, wk, stag, wkb, 8, D)
    load_cast_w(p, wv, stag, wvb, 8, D)
    p.dma(pjs[0:64, :], projT[:, :])
    p.dma(pjs[64:128, :], projT[:, :])
    p.emit("vector", lambda e: e.tensor_scalar_mul(pjb[:, :], pjs[:, :], DN))
    p.emit("vector", lambda e: e.memset(kmc[:, :], NEG))

    for t in range(TIL):
        p.dma(xt[:, :], ln[t * 128:(t + 1) * 128, :])
        p.emit("vector", lambda e: e.tensor_copy(xnb[:, :], xt[:, :]))
        tr_in(p, xnb, xnT, pT, idb, 8)
        lhs = [xnT[:, k * 128:(k + 1) * 128] for k in range(8)]
        for (wsb, isq, isk) in ((wqb, True, False), (wkb, True, True), (wvb, False, False)):
            for c0 in range(0, D, 512):
                rhs = [wsb[:, k * D + c0: k * D + c0 + 512] for k in range(8)]
                mm_acc(p, pA[:, :512], lhs, rhs)
                p.emit("vector", lambda e, c0=c0: e.tensor_copy(qf[:, c0:c0 + 512],
                                                                pA[:, :512]))
            if not isq:
                p.emit("vector", lambda e: e.tensor_copy(qb[:, :], qf[:, :]))
                p.dma(o_v[t * 128:(t + 1) * 128, :], qb[:, :])
                continue
            col0 = 16 if isk else 0
            p.emit("scalar", lambda e: e.activation(sq[:, :], qf[:, :], AF.Square))
            sq3 = sq.rearrange("p (a b) -> p a b", b=64)
            p.emit("vector", lambda e, col0=col0: e.reduce_sum(dg[:, col0:col0 + 16],
                                                               sq3[:, :, :], axis=AX))
            p.emit("vector", lambda e, col0=col0: e.tensor_scalar_mul(
                dg[:, col0:col0 + 16], dg[:, col0:col0 + 16], DN * DN * 0.5))
            p.emit("vector", lambda e: e.tensor_copy(qb[:, :], qf[:, :]))
            for k in range(8):
                p.emit("tensor", lambda e, k=k: e.transpose(pT[:, :128],
                                                            qb[:, k * 128:(k + 1) * 128], idb[:, :]))
                p.emit("vector", lambda e, k=k: e.tensor_copy(qT[:, k * 128:(k + 1) * 128],
                                                              pT[:, :128]))
            for h in range(16):
                par, pr = h % 2, h // 2
                p.group("tensor", [lambda e, par=par, pr=pr: e.matmul(
                    pH[:, 0:64], qT[par * 64:(par + 1) * 64, pr * 128:(pr + 1) * 128],
                    pjb[par * 64:(par + 1) * 64, :], start=True, stop=True)])
                p.emit("vector", lambda e, h=h: e.tensor_copy(feat[:, h * 64:(h + 1) * 64],
                                                              pH[:, 0:64]))
            if isk:
                p.emit("vector", lambda e: e.reduce_max(rmx[:, :], feat[:, :], axis=AX))
                p.emit("vector", lambda e: e.tensor_max(kmc[:, :], kmc[:, :], rmx[:, :]))
                p.dma(o_kd[t * 128:(t + 1) * 128, :], feat[:, :])
            else:
                p.dma(o_qd[t * 128:(t + 1) * 128, :], feat[:, :])
        p.dma(o_diag[t * 128:(t + 1) * 128, :], dg[:, :])
    p.group("tensor", [lambda e: e.transpose(pTF[0:1, 0:128], kmc[:, 0:1], idf[:, :])])
    p.emit("vector", lambda e: e.tensor_copy(krow[0:1, :], pTF[0:1, 0:128]))
    p.emit("vector", lambda e: e.reduce_max(kmx[0:1, 0:1], krow[0:1, :], axis=AX))
    p.dma(o_kmax[0:1, 0:1], kmx[0:1, 0:1])
    return p.build()


def build_S10():
    p = P()
    kd = p.inp("kd", [NT, D], F32)
    dgk = p.inp("dgk", [NT, 16], F32)
    v = p.inp("v", [NT, D], BF16)
    kmaxs = p.inp("kmaxs", [1, 8], F32)
    o_kvp = p.outp("kvp", [D, 65], F32)

    cm = common_alloc(p)
    idb, ones1 = cm["idb"], cm["ones1"]
    kms = p.sbuf("kms", [1, 8], F32)
    mxb = p.sbuf("mxb", [1, 1], BF16)
    nmx = p.sbuf("nmx", [128, 1], F32)
    kdt = p.sbuf("kdt", [128, D], F32)
    dgt = p.sbuf("dgt", [128, 16], F32)
    ngb = p.sbuf("ngb", [128, 16], F32)
    kfb_all = p.sbuf("kfb_all", [128, TIL * D], BF16)
    vb_all = p.sbuf("vb_all", [128, TIL * D], BF16)
    kft = p.sbuf("kft", [128, D], F32)
    ones128 = p.sbuf("ones128", [128, 1], BF16)
    kvacc = p.sbuf("kvacc", [128, 8 * 65], F32)
    pE = p.psum("pE", [128, 128], F32)
    pB = p.psum("pB", [128, 128], F32)

    RATIO = 0.125
    EPSK = 1e-6
    p.dma(kms[:, :], kmaxs[:, :])
    p.emit("vector", lambda e: e.reduce_max(kms[0:1, 0:1], kms[0:1, :], axis=AX))
    p.emit("scalar", lambda e: e.mul(mxb[0:1, 0:1], kms[0:1, 0:1], -1.0))
    p.group("tensor", [lambda e: e.matmul(pE[:, 0:1], ones1[0:1, :], mxb[0:1, 0:1],
                                          start=True, stop=True)])
    p.emit("vector", lambda e: e.tensor_copy(nmx[:, 0:1], pE[:, 0:1]))
    p.emit("vector", lambda e: e.memset(ones128[:, :], 1.0))
    p.emit("vector", lambda e: e.memset(kvacc[:, :], 0.0))
    for t in range(TIL):
        p.dma(kdt[:, :], kd[t * 128:(t + 1) * 128, :])
        p.dma(dgt[:, :], dgk[t * 128:(t + 1) * 128, :])
        p.dma(vb_all[:, t * D:(t + 1) * D], v[t * 128:(t + 1) * 128, :])
        p.emit("vector", lambda e: e.tensor_scalar(ngb[:, :], dgt[:, :], nmx[:, 0:1],
                                                   -1.0, op0=OP.subtract, op1=OP.mult))
        for h in range(16):
            p.emit("scalar", lambda e, h=h: e.activation(
                kft[:, h * 64:(h + 1) * 64], kdt[:, h * 64:(h + 1) * 64], AF.Exp,
                bias=ngb[:, h:h + 1]))
        p.emit("vector", lambda e, t=t: e.tensor_scalar(
            kfb_all[:, t * D:(t + 1) * D], kft[:, :], RATIO, RATIO * EPSK,
            op0=OP.mult, op1=OP.add))
    for pr in range(8):
        h0, h1 = 2 * pr, 2 * pr + 1
        fns = []
        for t in range(TIL):
            for (h, base) in ((h0, 0), (h1, 64)):
                fns.append(lambda e, t=t, h=h, base=base: e.matmul(
                    pB[base:base + 64, 0:64], kfb_all[:, t * D + h * 64: t * D + (h + 1) * 64],
                    vb_all[:, t * D + h * 64: t * D + (h + 1) * 64],
                    start=(t == 0), stop=(t == TIL - 1)))
                fns.append(lambda e, t=t, h=h, base=base: e.matmul(
                    pB[base:base + 64, 64:65], kfb_all[:, t * D + h * 64: t * D + (h + 1) * 64],
                    ones128[:, :], start=(t == 0), stop=(t == TIL - 1)))
        p.group("tensor", fns)
        p.emit("vector", lambda e, pr=pr: e.tensor_copy(kvacc[:, pr * 65:(pr + 1) * 65],
                                                        pB[:, 0:65]))
    o_kvp_r = o_kvp.rearrange("(a b) c -> b a c", b=128)
    p.dma(o_kvp_r, kvacc.rearrange("b (a c) -> b a c", a=8))
    return p.build()


def build_S11a():
    p = P()
    qd = p.inp("qd", [NT, D], F32)
    dgq = p.inp("dgq", [NT, 16], F32)
    lnx = p.inp("lnx", [NT, D], F32)
    kvps = [p.inp(f"kvp{i}", [D, 65], F32) for i in range(4)]
    wof = p.inp("wof", [D, D], F32)
    g1 = p.inp("g1", [1, D], F32)
    be1 = p.inp("be1", [1, D], F32)
    o_o1 = p.outp("o1", [NT, D], F32)

    cm = common_alloc(p)
    idb, ones1 = cm["idb"], cm["ones1"]
    stag = p.sbuf("stag", [128, D], F32)
    wob = p.sbuf("wob", [128, 8 * D], BF16)
    srow = p.sbuf("srow", [1, D], F32)
    g1r = p.sbuf("g1r", [1, D], BF16)
    be1r = p.sbuf("be1r", [1, D], BF16)
    g1b = p.sbuf("g1b", [128, D], F32)
    be1b = p.sbuf("be1b", [128, D], F32)
    kvs = p.sbuf("kvs", [128, 8 * 65], F32)
    kvt = p.sbuf("kvt", [128, 8 * 65], F32)
    kvb = p.sbuf("kvb", [128, 8 * 65], BF16)
    qdt = p.sbuf("qdt", [128, D], F32)
    dgt = p.sbuf("dgt", [128, 16], F32)
    rmax = p.sbuf("rmax", [128, 16], F32)
    ngb = p.sbuf("ngb", [128, 16], F32)
    qfb = p.sbuf("qfb", [128, D], BF16)
    qfT = p.sbuf("qfT", [128, D], BF16)
    att = p.sbuf("att", [128, D], F32)
    attb = p.sbuf("attb", [128, D], BF16)
    attT = p.sbuf("attT", [128, D], BF16)
    den = p.sbuf("den", [128, 16], F32)
    xt = p.sbuf("xt", [128, D], F32)
    t1 = p.sbuf("t1", [128, D], F32)
    o1 = p.sbuf("o1sa", [128, D], F32)
    scr = p.sbuf("scr", [128, D], F32)
    st1 = p.sbuf("st1", [128, 2], F32)
    st2 = p.sbuf("st2", [128, 2], F32)
    pA = p.psum("pA", [128, 512], F32)
    pC = p.psum("pC", [128, 128], F32)
    pT = p.psum("pT", [128, 512], BF16)

    RATIO = 0.125
    EPSK = 1e-6
    load_cast_w(p, wof, stag, wob, 8, D)
    load_row_bf(p, g1[0:1, :], srow, g1r, D)
    load_row_bf(p, be1[0:1, :], srow, be1r, D)
    bcast_row(p, ones1, g1r, pA, g1b, D)
    bcast_row(p, ones1, be1r, pA, be1b, D)
    for i in range(4):
        dst = kvs if i == 0 else kvt
        p.dma(dst.rearrange("b (a c) -> b a c", a=8),
              kvps[i].rearrange("(a b) c -> b a c", b=128))
        if i > 0:
            p.emit("vector", lambda e: e.tensor_add(kvs[:, :], kvs[:, :], kvt[:, :]))
    p.emit("vector", lambda e: e.tensor_copy(kvb[:, :], kvs[:, :]))

    for t in range(TIL):
        p.dma(qdt[:, :], qd[t * 128:(t + 1) * 128, :])
        p.dma(dgt[:, :], dgq[t * 128:(t + 1) * 128, :])
        qd3 = qdt.rearrange("p (a b) -> p a b", b=64)
        p.emit("vector", lambda e: e.reduce_max(rmax[:, :], qd3[:, :, :], axis=AX))
        p.emit("vector", lambda e: e.tensor_add(ngb[:, :], dgt[:, :], rmax[:, :]))
        p.emit("vector", lambda e: e.tensor_scalar_mul(ngb[:, :], ngb[:, :], -1.0))
        for h in range(16):
            p.emit("scalar", lambda e, h=h: e.activation(
                att[:, h * 64:(h + 1) * 64], qdt[:, h * 64:(h + 1) * 64], AF.Exp,
                bias=ngb[:, h:h + 1]))
        p.emit("vector", lambda e: e.tensor_scalar(qfb[:, :], att[:, :], RATIO,
                                                   RATIO * EPSK, op0=OP.mult, op1=OP.add))
        tr_in(p, qfb, qfT, pT, idb, 8)
        for h in range(16):
            par, pr = h % 2, h // 2
            p.group("tensor", [lambda e, par=par, pr=pr: e.matmul(
                pC[:, 0:65], qfT[par * 64:(par + 1) * 64, pr * 128:(pr + 1) * 128],
                kvb[par * 64:(par + 1) * 64, pr * 65:(pr + 1) * 65], start=True, stop=True)])
            p.emit("vector", lambda e, h=h: e.tensor_copy(att[:, h * 64:(h + 1) * 64],
                                                          pC[:, 0:64]))
            p.emit("vector", lambda e, h=h: e.tensor_copy(den[:, h:h + 1], pC[:, 64:65]))
        p.emit("vector", lambda e: e.reciprocal(den[:, :], den[:, :]))
        for h in range(16):
            p.emit("vector", lambda e, h=h: e.tensor_scalar_mul(
                attb[:, h * 64:(h + 1) * 64], att[:, h * 64:(h + 1) * 64], den[:, h:h + 1]))
        tr_in(p, attb, attT, pT, idb, 8)
        p.dma(xt[:, :], lnx[t * 128:(t + 1) * 128, :])
        lhs = [attT[:, k * 128:(k + 1) * 128] for k in range(8)]
        for c0 in range(0, D, 512):
            rhs = [wob[:, k * D + c0: k * D + c0 + 512] for k in range(8)]
            mm_acc(p, pA[:, :512], lhs, rhs)
            p.emit("vector", lambda e, c0=c0: e.tensor_add(t1[:, c0:c0 + 512],
                                                           pA[:, :512], xt[:, c0:c0 + 512]))
        ln_stats(p, t1, st1, st2, scr, D, eps=1e-6)
        p.emit("vector", lambda e: e.tensor_scalar(o1[:, :], t1[:, :], st1[:, 0:1],
                                                   st2[:, 0:1], op0=OP.add, op1=OP.mult))
        p.emit("vector", lambda e: e.tensor_mul(o1[:, :], o1[:, :], g1b[:, :]))
        p.emit("vector", lambda e: e.tensor_add(o1[:, :], o1[:, :], be1b[:, :]))
        p.dma(o_o1[t * 128:(t + 1) * 128, :], o1[:, :])
    return p.build()


def build_S11b():
    p = P()
    o1i = p.inp("o1", [NT, D], F32)
    x4 = p.inp("x4", [NT, D], F32)
    w1 = p.inp("w1", [D, 4096], F32)
    b1 = p.inp("b1", [1, 4096], F32)
    w2 = p.inp("w2", [4096, D], F32)
    b2 = p.inp("b2", [1, D], F32)
    g2 = p.inp("g2", [1, D], F32)
    be2 = p.inp("be2", [1, D], F32)
    o_x5 = p.outp("x5", [NT, D], F32)

    cm = common_alloc(p)
    idb, ones1 = cm["idb"], cm["ones1"]
    stag = p.sbuf("stag", [128, 1024], F32)
    w1b = p.sbuf("w1b", [128, 8 * 4096], BF16)
    w2b = p.sbuf("w2b", [128, 32 * D], BF16)
    srow = p.sbuf("srow", [1, 4096], F32)
    b1r = p.sbuf("b1r", [1, 4096], BF16)
    b2r = p.sbuf("b2r", [1, D], BF16)
    g2r = p.sbuf("g2r", [1, D], BF16)
    be2r = p.sbuf("be2r", [1, D], BF16)
    g2b = p.sbuf("g2b", [128, D], F32)
    be2b = p.sbuf("be2b", [128, D], F32)
    o1 = p.sbuf("o1s", [128, D], F32)
    o1b = p.sbuf("o1b", [128, D], BF16)
    o1T = p.sbuf("o1T", [128, D], BF16)
    hb = p.sbuf("hb", [128, 512], BF16)
    hT = p.sbuf("hT", [128, 4096], BF16)
    e_ = p.sbuf("e_", [128, 512], F32)
    r_ = p.sbuf("r_", [128, 512], F32)
    t1 = p.sbuf("t1", [128, D], F32)
    scr = p.sbuf("scr", [128, D], F32)
    st1 = p.sbuf("st1", [128, 2], F32)
    st2 = p.sbuf("st2", [128, 2], F32)
    xt = p.sbuf("xt", [128, D], F32)
    res = p.sbuf("res", [128, D], F32)
    pA = p.psum("pA", [128, 512], F32)
    pT = p.psum("pT", [128, 512], BF16)

    for k in range(8):
        for s4 in range(4):
            p.dma(stag[:, :1024], w1[k * 128:(k + 1) * 128, s4 * 1024:(s4 + 1) * 1024])
            p.emit("vector", lambda e, k=k, s4=s4: e.tensor_copy(
                w1b[:, k * 4096 + s4 * 1024: k * 4096 + (s4 + 1) * 1024], stag[:, :1024]))
    for k in range(32):
        p.dma(stag[:, :D], w2[k * 128:(k + 1) * 128, :])
        p.emit("vector", lambda e, k=k: e.tensor_copy(w2b[:, k * D:(k + 1) * D],
                                                      stag[:, :D]))
    load_row_bf(p, b1[0:1, :], srow, b1r, 4096)
    load_row_bf(p, b2[0:1, :], srow, b2r, D)
    load_row_bf(p, g2[0:1, :], srow, g2r, D)
    load_row_bf(p, be2[0:1, :], srow, be2r, D)
    bcast_row(p, ones1, g2r, pA, g2b, D)
    bcast_row(p, ones1, be2r, pA, be2b, D)

    for t in range(TIL):
        p.dma(o1[:, :], o1i[t * 128:(t + 1) * 128, :])
        p.emit("vector", lambda e: e.tensor_copy(o1b[:, :], o1[:, :]))
        tr_in(p, o1b, o1T, pT, idb, 8)
        lhs2 = [o1T[:, k * 128:(k + 1) * 128] for k in range(8)]
        for c0 in range(0, 4096, 512):
            rhs = [w1b[:, k * 4096 + c0: k * 4096 + c0 + 512] for k in range(8)]
            mm_acc(p, pA[:, :512], lhs2, rhs, bias=(ones1[0:1, :], b1r[0:1, c0:c0 + 512]))
            p.emit("vector", lambda e: e.tensor_scalar_min(e_[:, :], pA[:, :512], 0.0))
            p.emit("scalar", lambda e: e.activation(e_[:, :], e_[:, :], AF.Exp))
            p.emit("vector", lambda e: e.tensor_scalar_max(r_[:, :], pA[:, :512], 0.0))
            p.emit("vector", lambda e: e.scalar_tensor_tensor(
                hb[:, :], r_[:, :], -1.0, e_[:, :], op0=OP.add, op1=OP.add))
            for j in range(4):
                p.emit("tensor", lambda e, j=j: e.transpose(
                    pT[:, j * 128:(j + 1) * 128], hb[:, j * 128:(j + 1) * 128], idb[:, :]))
            p.emit("vector", lambda e, c0=c0: e.tensor_copy(hT[:, c0:c0 + 512],
                                                            pT[:, :512]))
        lhs3 = [hT[:, k * 128:(k + 1) * 128] for k in range(32)]
        for c0 in range(0, D, 512):
            rhs = [w2b[:, k * D + c0: k * D + c0 + 512] for k in range(32)]
            mm_acc(p, pA[:, :512], lhs3, rhs, bias=(ones1[0:1, :], b2r[0:1, c0:c0 + 512]))
            p.emit("vector", lambda e, c0=c0: e.tensor_add(t1[:, c0:c0 + 512],
                                                           pA[:, :512], o1[:, c0:c0 + 512]))
        ln_stats(p, t1, st1, st2, scr, D, eps=1e-6)
        p.emit("vector", lambda e: e.tensor_scalar(scr[:, :], t1[:, :], st1[:, 0:1],
                                                   st2[:, 0:1], op0=OP.add, op1=OP.mult))
        p.emit("vector", lambda e: e.tensor_mul(scr[:, :], scr[:, :], g2b[:, :]))
        p.emit("vector", lambda e: e.tensor_add(scr[:, :], scr[:, :], be2b[:, :]))
        p.dma(xt[:, :], x4[t * 128:(t + 1) * 128, :])
        p.emit("vector", lambda e: e.tensor_add(res[:, :], scr[:, :], xt[:, :]))
        p.dma(o_x5[t * 128:(t + 1) * 128, :], res[:, :])
    return p.build()


def build_S12():
    p = P()
    lnh = p.inp("lnh", [NT + 1, D], F32)
    x5 = p.inp("x5", [NT, D], F32)
    wrT = p.inp("wrT", [D, D], F32)
    wkT = p.inp("wkT", [D, D], F32)
    wvT = p.inp("wvT", [D, D], F32)
    mur = p.inp("mur", [1, D], F32)
    muk = p.inp("muk", [1, D], F32)
    o_x6 = p.outp("x6", [NT, D], F32)

    cm = common_alloc(p)
    idb, ones1 = cm["idb"], cm["ones1"]
    stag = p.sbuf("stag", [128, D], F32)
    wrb = p.sbuf("wrb", [128, 8 * D], BF16)
    wkb = p.sbuf("wkb", [128, 8 * D], BF16)
    wvb = p.sbuf("wvb", [128, 8 * D], BF16)
    srow = p.sbuf("srow", [1, D], F32)
    mrr = p.sbuf("mrr", [1, D], BF16)
    mkr = p.sbuf("mkr", [1, D], BF16)
    mrb = p.sbuf("mrb", [128, D], F32)
    mkb = p.sbuf("mkb", [128, D], F32)
    lt = p.sbuf("lt", [128, D], F32)
    lp = p.sbuf("lp", [128, D], F32)
    dt = p.sbuf("dt", [128, D], F32)
    xr = p.sbuf("xr", [128, D], BF16)
    xk = p.sbuf("xk", [128, D], BF16)
    xrT = p.sbuf("xrT", [128, D], BF16)
    xkT = p.sbuf("xkT", [128, D], BF16)
    r_ = p.sbuf("r_", [128, D], F32)
    kab = p.sbuf("kab", [128, D], BF16)
    kaT = p.sbuf("kaT", [128, D], BF16)
    tmp = p.sbuf("tmp", [128, 512], F32)
    xt = p.sbuf("xt", [128, D], F32)
    res = p.sbuf("res", [128, D], F32)
    pA = p.psum("pA", [128, 512], F32)
    pT = p.psum("pT", [128, 512], BF16)

    load_cast_w(p, wrT, stag, wrb, 8, D)
    load_cast_w(p, wkT, stag, wkb, 8, D)
    load_cast_w(p, wvT, stag, wvb, 8, D)
    load_row_bf(p, mur[0:1, :], srow, mrr, D)
    load_row_bf(p, muk[0:1, :], srow, mkr, D)
    bcast_row(p, ones1, mrr, pA, mrb, D)
    bcast_row(p, ones1, mkr, pA, mkb, D)

    for t in range(TIL):
        p.dma(lt[:, :], lnh[1 + t * 128: 1 + (t + 1) * 128, :])
        p.dma(lp[:, :], lnh[t * 128: (t + 1) * 128, :])
        p.emit("vector", lambda e: e.tensor_sub(dt[:, :], lt[:, :], lp[:, :]))
        p.emit("vector", lambda e: e.tensor_mul(res[:, :], dt[:, :], mrb[:, :]))
        p.emit("vector", lambda e: e.tensor_add(xr[:, :], res[:, :], lp[:, :]))
        p.emit("vector", lambda e: e.tensor_mul(res[:, :], dt[:, :], mkb[:, :]))
        p.emit("vector", lambda e: e.tensor_add(xk[:, :], res[:, :], lp[:, :]))
        tr_in(p, xr, xrT, pT, idb, 8)
        tr_in(p, xk, xkT, pT, idb, 8)
        lhsr = [xrT[:, k * 128:(k + 1) * 128] for k in range(8)]
        lhsk = [xkT[:, k * 128:(k + 1) * 128] for k in range(8)]
        for c0 in range(0, D, 512):
            rhs = [wrb[:, k * D + c0: k * D + c0 + 512] for k in range(8)]
            mm_acc(p, pA[:, :512], lhsr, rhs)
            p.emit("scalar", lambda e, c0=c0: e.activation(r_[:, c0:c0 + 512], pA[:, :512],
                                                           AF.Sigmoid))
        for c0 in range(0, D, 512):
            rhs = [wkb[:, k * D + c0: k * D + c0 + 512] for k in range(8)]
            mm_acc(p, pA[:, :512], lhsk, rhs)
            p.emit("vector", lambda e: e.tensor_scalar_max(tmp[:, :], pA[:, :512], 0.0))
            p.emit("scalar", lambda e, c0=c0: e.activation(kab[:, c0:c0 + 512], tmp[:, :],
                                                           AF.Square))
        tr_in(p, kab, kaT, pT, idb, 8)
        lhsv = [kaT[:, k * 128:(k + 1) * 128] for k in range(8)]
        p.dma(xt[:, :], x5[t * 128:(t + 1) * 128, :])
        for c0 in range(0, D, 512):
            rhs = [wvb[:, k * D + c0: k * D + c0 + 512] for k in range(8)]
            mm_acc(p, pA[:, :512], lhsv, rhs)
            p.emit("vector", lambda e, c0=c0: e.tensor_mul(res[:, c0:c0 + 512],
                                                           pA[:, :512], r_[:, c0:c0 + 512]))
            p.emit("vector", lambda e, c0=c0: e.tensor_add(res[:, c0:c0 + 512],
                                                           res[:, c0:c0 + 512],
                                                           xt[:, c0:c0 + 512]))
        p.dma(o_x6[t * 128:(t + 1) * 128, :], res[:, :])
    return p.build()


def build_S13():
    p = P()
    ln = p.inp("ln", [NT, D], F32)
    wzT = p.inp("wzT", [D, D], F32)
    whT = p.inp("whT", [D, D], F32)
    bz = p.inp("bz", [1, D], F32)
    bh = p.inp("bh", [1, D], F32)
    idf_in = p.inp("identf", [128, 128], F32)
    o_aT = p.outp("aT", [8, 128, NT], F32)
    o_bT = p.outp("bT", [8, 128, NT], F32)

    cm = common_alloc(p)
    idb, ones1 = cm["idb"], cm["ones1"]
    idf = p.sbuf("idf", [128, 128], F32)
    p.dma(idf[:, :], idf_in[:, :])
    stag = p.sbuf("stag", [128, D], F32)
    wzb = p.sbuf("wzb", [128, 8 * D], BF16)
    whb = p.sbuf("whb", [128, 8 * D], BF16)
    srow = p.sbuf("srow", [1, D], F32)
    bzr = p.sbuf("bzr", [1, D], BF16)
    bhr = p.sbuf("bhr", [1, D], BF16)
    xt = p.sbuf("xt", [128, D], F32)
    xb = p.sbuf("xb", [128, D], BF16)
    xT = p.sbuf("xT", [128, D], BF16)
    z_ = p.sbuf("z_", [128, D], F32)
    ht = p.sbuf("ht", [128, D], F32)
    a_ = p.sbuf("a_", [128, D], F32)
    b_ = p.sbuf("b_", [128, D], F32)
    aT = p.sbuf("aTs", [128, D], F32)
    bT = p.sbuf("bTs", [128, D], F32)
    pA = p.psum("pA", [128, 512], F32)
    pT = p.psum("pT", [128, 512], BF16)
    pTF = p.psum("pTF", [128, 512], F32)

    load_cast_w(p, wzT, stag, wzb, 8, D)
    load_cast_w(p, whT, stag, whb, 8, D)
    load_row_bf(p, bz[0:1, :], srow, bzr, D)
    load_row_bf(p, bh[0:1, :], srow, bhr, D)

    for t in range(TIL):
        p.dma(xt[:, :], ln[t * 128:(t + 1) * 128, :])
        p.emit("vector", lambda e: e.tensor_copy(xb[:, :], xt[:, :]))
        tr_in(p, xb, xT, pT, idb, 8)
        lhs = [xT[:, k * 128:(k + 1) * 128] for k in range(8)]
        for c0 in range(0, D, 512):
            rhs = [wzb[:, k * D + c0: k * D + c0 + 512] for k in range(8)]
            mm_acc(p, pA[:, :512], lhs, rhs, bias=(ones1[0:1, :], bzr[0:1, c0:c0 + 512]))
            p.emit("scalar", lambda e, c0=c0: e.activation(z_[:, c0:c0 + 512], pA[:, :512],
                                                           AF.Sigmoid))
        for c0 in range(0, D, 512):
            rhs = [whb[:, k * D + c0: k * D + c0 + 512] for k in range(8)]
            mm_acc(p, pA[:, :512], lhs, rhs, bias=(ones1[0:1, :], bhr[0:1, c0:c0 + 512]))
            p.emit("vector", lambda e, c0=c0: e.tensor_copy(ht[:, c0:c0 + 512], pA[:, :512]))
        p.emit("vector", lambda e: e.tensor_scalar(a_[:, :], z_[:, :], -1.0, 1.0 + 1e-8,
                                                   op0=OP.mult, op1=OP.add))
        p.emit("vector", lambda e: e.tensor_mul(b_[:, :], z_[:, :], ht[:, :]))
        for (src, dstT, out) in ((a_, aT, o_aT), (b_, bT, o_bT)):
            for k in range(0, 8, 4):
                for j in range(4):
                    p.emit("tensor", lambda e, k=k, j=j, src=src: e.transpose(
                        pTF[:, j * 128:(j + 1) * 128],
                        src[:, (k + j) * 128:(k + j + 1) * 128], idf[:, :]))
                p.emit("vector", lambda e, k=k, dstT=dstT: e.tensor_copy(
                    dstT[:, k * 128:(k + 4) * 128], pTF[:, :512]))
            for k in range(8):
                p.dma(out[k, :, t * 128:(t + 1) * 128], dstT[:, k * 128:(k + 1) * 128])
    return p.build()


def build_S14():
    p = P()
    aT = p.inp("aT", [2, 128, T], F32)
    bT = p.inp("bT", [2, 128, T], F32)
    idf_in = p.inp("identf", [128, 128], F32)
    o_h = p.outp("ho", [2, T, 128], F32)

    idf = p.sbuf("idf", [128, 128], F32)
    p.dma(idf[:, :], idf_in[:, :])
    asb = p.sbuf("asb", [128, T], F32)
    bsb = p.sbuf("bsb", [128, T], F32)
    hsb = p.sbuf("hsb", [128, T], F32)
    hb4 = p.sbuf("hb4", [128, 512], F32)
    pTF = p.psum("pTF", [128, 512], F32)
    for u in range(2):
        p.dma(asb[:, :], aT[u, :, :])
        p.dma(bsb[:, :], bT[u, :, :])
        p.emit("vector", lambda e: e.tensor_tensor_scan(hsb[:, :], asb[:, :], bsb[:, :],
                                                        0.0, op0=OP.mult, op1=OP.add))
        for g in range(16):
            for j in range(4):
                p.emit("tensor", lambda e, g=g, j=j: e.transpose(
                    pTF[:, j * 128:(j + 1) * 128],
                    hsb[:, (4 * g + j) * 128:(4 * g + j + 1) * 128], idf[:, :]))
            p.emit("vector", lambda e: e.tensor_copy(hb4[:, :], pTF[:, :512]))
            for j in range(4):
                tt = 4 * g + j
                p.dma(o_h[u, tt * 128:(tt + 1) * 128, :], hb4[:, j * 128:(j + 1) * 128])
    return p.build()


def build_L7a():
    p = P()
    ln = p.inp("ln", [NT, D], F32)
    wg = p.inp("wg", [D, 8], F32)
    o_w = p.outp("w", [NT, 8], F32)

    cm = common_alloc(p)
    idb, ones1 = cm["idb"], cm["ones1"]
    stag = p.sbuf("stag", [128, 8], F32)
    wgb = p.sbuf("wgb", [128, 8 * 8], BF16)
    xt = p.sbuf("xt", [128, D], F32)
    xb = p.sbuf("xb", [128, D], BF16)
    xT = p.sbuf("xT", [128, D], BF16)
    g = p.sbuf("g", [128, 8], F32)
    m1 = p.sbuf("m1", [128, 1], F32)
    m2 = p.sbuf("m2", [128, 1], F32)
    i1 = p.sbuf("i1", [128, 8], F32)
    i2 = p.sbuf("i2", [128, 8], F32)
    g2 = p.sbuf("g2", [128, 8], F32)
    s_ = p.sbuf("s_", [128, 1], F32)
    w_ = p.sbuf("w_", [128, 8], F32)
    pA = p.psum("pA", [128, 8], F32)
    pT = p.psum("pT", [128, 512], BF16)
    load_cast_w(p, wg, stag, wgb, 8, 8)
    for t in range(TIL):
        p.dma(xt[:, :], ln[t * 128:(t + 1) * 128, :])
        p.emit("vector", lambda e: e.tensor_copy(xb[:, :], xt[:, :]))
        tr_in(p, xb, xT, pT, idb, 8)
        lhs = [xT[:, k * 128:(k + 1) * 128] for k in range(8)]
        rhs = [wgb[:, k * 8:(k + 1) * 8] for k in range(8)]
        mm_acc(p, pA[:, 0:8], lhs, rhs)
        p.emit("scalar", lambda e: e.activation(g[:, :], pA[:, 0:8], AF.Sigmoid))
        p.emit("vector", lambda e: e.reduce_max(m1[:, :], g[:, :], axis=AX))
        p.emit("vector", lambda e: e.tensor_scalar(i1[:, :], g[:, :], m1[:, 0:1], None,
                                                   op0=OP.is_ge))
        p.emit("vector", lambda e: e.scalar_tensor_tensor(g2[:, :], i1[:, :], -1e30,
                                                          g[:, :], op0=OP.mult, op1=OP.add))
        p.emit("vector", lambda e: e.reduce_max(m2[:, :], g2[:, :], axis=AX))
        p.emit("vector", lambda e: e.tensor_scalar(i2[:, :], g2[:, :], m2[:, 0:1], None,
                                                   op0=OP.is_ge))
        p.emit("vector", lambda e: e.tensor_add(s_[:, :], m1[:, :], m2[:, :]))
        p.emit("vector", lambda e: e.reciprocal(s_[:, :], s_[:, :]))
        p.emit("vector", lambda e: e.tensor_scalar_mul(i1[:, :], i1[:, :], m1[:, 0:1]))
        p.emit("vector", lambda e: e.tensor_scalar_mul(i2[:, :], i2[:, :], m2[:, 0:1]))
        p.emit("vector", lambda e: e.tensor_add(w_[:, :], i1[:, :], i2[:, :]))
        p.emit("vector", lambda e: e.tensor_scalar_mul(w_[:, :], w_[:, :], s_[:, 0:1]))
        p.dma(o_w[t * 128:(t + 1) * 128, :], w_[:, :])
    return p.build()


def build_L7b():
    p = P()
    ln = p.inp("ln", [NT, D], F32)
    wcol = p.inp("wcol", [NT, 1], F32)
    we1 = p.inp("we1", [D, 2048], F32)
    we3 = p.inp("we3", [D, 2048], F32)
    we2 = p.inp("we2", [2048, D], F32)
    acc = p.inp("acc", [NT, D], F32)
    o_acc = p.outp("acc_o", [NT, D], F32)

    cm = common_alloc(p)
    idb, ones1 = cm["idb"], cm["ones1"]
    stag = p.sbuf("stag", [128, 2048], F32)
    w1b = p.sbuf("w1b", [128, 8 * 2048], BF16)
    w3b = p.sbuf("w3b", [128, 8 * 2048], BF16)
    w2b = p.sbuf("w2b", [128, 16 * D], BF16)
    xt = p.sbuf("xt", [128, D], F32)
    xb = p.sbuf("xb", [128, D], BF16)
    xT = p.sbuf("xT", [128, D], BF16)
    wc = p.sbuf("wc", [128, 1], F32)
    s1 = p.sbuf("s1", [128, 512], F32)
    pr_ = p.sbuf("pr_", [128, 512], F32)
    pb = p.sbuf("pb", [128, 2048], BF16)
    pbT = p.sbuf("pbT", [128, 2048], BF16)
    at = p.sbuf("at", [128, D], F32)
    pA = p.psum("pA", [128, 512], F32)
    pB = p.psum("pB", [128, 512], F32)
    pC = p.psum("pC", [128, 512], F32)
    pD = p.psum("pD", [128, 512], F32)
    pT = p.psum("pT", [128, 512], BF16)

    for k in range(8):
        p.dma(stag[:, :], we1[k * 128:(k + 1) * 128, :])
        p.emit("vector", lambda e, k=k: e.tensor_copy(w1b[:, k * 2048:(k + 1) * 2048],
                                                      stag[:, :]))
        p.dma(stag[:, :], we3[k * 128:(k + 1) * 128, :])
        p.emit("vector", lambda e, k=k: e.tensor_copy(w3b[:, k * 2048:(k + 1) * 2048],
                                                      stag[:, :]))
    for k in range(16):
        p.dma(stag[:, :D], we2[k * 128:(k + 1) * 128, :])
        p.emit("vector", lambda e, k=k: e.tensor_copy(w2b[:, k * D:(k + 1) * D],
                                                      stag[:, :D]))
    for t in range(TIL):
        p.dma(xt[:, :], ln[t * 128:(t + 1) * 128, :])
        p.dma(wc[:, 0:1], wcol[t * 128:(t + 1) * 128, :])
        p.emit("vector", lambda e: e.tensor_copy(xb[:, :], xt[:, :]))
        tr_in(p, xb, xT, pT, idb, 8)
        lhs = [xT[:, k * 128:(k + 1) * 128] for k in range(8)]
        for c0 in range(0, 2048, 512):
            rhs1 = [w1b[:, k * 2048 + c0: k * 2048 + c0 + 512] for k in range(8)]
            rhs3 = [w3b[:, k * 2048 + c0: k * 2048 + c0 + 512] for k in range(8)]
            mm_acc(p, pA[:, :512], lhs, rhs1)
            mm_acc(p, pB[:, :512], lhs, rhs3)
            p.emit("scalar", lambda e: e.activation(s1[:, :], pA[:, :512], AF.Silu))
            p.emit("vector", lambda e: e.tensor_mul(pr_[:, :], pB[:, :512], s1[:, :]))
            p.emit("vector", lambda e, c0=c0: e.tensor_scalar_mul(pb[:, c0:c0 + 512],
                                                                  pr_[:, :], wc[:, 0:1]))
        tr_in(p, pb, pbT, pT, idb, 16)
        fnsC, fnsD = [], []
        for k in range(16):
            fnsC.append(lambda e, k=k: e.matmul(pC[:, :512], pbT[:, k * 128:(k + 1) * 128],
                                                w2b[:, k * D: k * D + 512],
                                                start=(k == 0), stop=(k == 15)))
            fnsD.append(lambda e, k=k: e.matmul(pD[:, :512], pbT[:, k * 128:(k + 1) * 128],
                                                w2b[:, k * D + 512: k * D + 1024],
                                                start=(k == 0), stop=(k == 15)))
        p.group("tensor", fnsC)
        p.group("tensor", fnsD)
        p.dma(at[:, :], acc[t * 128:(t + 1) * 128, :])
        p.emit("vector", lambda e: e.tensor_add(at[:, 0:512], at[:, 0:512], pC[:, :512]))
        p.emit("vector", lambda e: e.tensor_add(at[:, 512:D], at[:, 512:D], pD[:, :512]))
        p.dma(o_acc[t * 128:(t + 1) * 128, :], at[:, :])
    return p.build()


def build_L7c():
    p = P()
    x7 = p.inp("x7", [NT, D], F32)
    acc = p.inp("acc", [NT, D], F32)
    o = p.outp("o", [NT, D], F32)
    xt = p.sbuf("xt", [128, D], F32)
    at = p.sbuf("at", [128, D], F32)
    for t in range(TIL):
        p.dma(xt[:, :], x7[t * 128:(t + 1) * 128, :])
        p.dma(at[:, :], acc[t * 128:(t + 1) * 128, :])
        p.emit("vector", lambda e: e.scalar_tensor_tensor(
            xt[:, :], at[:, :], 1.0 / 3.0, xt[:, :], op0=OP.mult, op1=OP.add))
        p.dma(o[t * 128:(t + 1) * 128, :], xt[:, :])
    return p.build()



def _run(nc, in_maps):
    r = run_bass_kernel_spmd(nc, in_maps, core_ids=list(range(NC)))
    return r.results


def _gather(results, key):
    return np.concatenate([np.asarray(r[key]) for r in results], axis=0)


def _tr(a):
    return np.ascontiguousarray(np.asarray(a).T)


def _f32(a):
    return np.ascontiguousarray(np.asarray(a), dtype=np.float32)


def _rotary_tables():
    DH, W = 64, 256
    half = np.arange(0, DH, 2, dtype=np.float64)
    inv = 1.0 / 10000.0 ** (half / DH)
    t = np.arange(T, dtype=np.float64)
    fr = t[:, None] * inv[None, :]
    freqs = np.concatenate([fr, fr], -1)
    base = (half + 0.4 * DH) / (1.4 * DH)
    power = (t - T // 2) / (W // 2)
    scale = base[None, :] ** power[:, None]
    scale = np.concatenate([scale, scale], -1)
    cs = (np.cos(freqs) * scale).astype(np.float32)
    ss = (np.sin(freqs) * scale).astype(np.float32)
    ci = (np.cos(freqs) / scale).astype(np.float32)
    si = (np.sin(freqs) / scale).astype(np.float32)
    return cs, ss, ci, si


def _l1_masks():
    w = 256
    out = np.zeros((4, 128, 512), np.float32)
    for variant in range(4):
        first = variant >= 2
        par = variant % 2
        it = par * 128 + np.arange(128)[:, None]
        jt = np.arange(512)[None, :] - 256
        bad = (it < jt) | (it > jt + w)
        if first:
            bad = bad | (np.arange(512)[None, :] < 256)
        out[variant] = np.where(bad, NEG, 0.0)
    return out


def _halo(xf, c, nh, zero_first):
    t0 = c * NT
    lo = t0 - nh
    if (c % 4) == 0 and zero_first:
        pad = np.zeros((nh, xf.shape[1]), xf.dtype)
        return np.concatenate([pad, xf[t0:t0 + NT]], 0)
    return np.ascontiguousarray(xf[lo:t0 + NT])


def kernel(**inputs):
    global TRACE
    ident_b = np.eye(128, dtype=BF)
    ident_f = np.eye(128, dtype=np.float32)

    xf = _f32(inputs["L0_attn_x"]).reshape(NTOK, D)

    def shards(a):
        return [np.ascontiguousarray(a[c * NT:(c + 1) * NT]) for c in range(NC)]

    nc = get_prog("S1", build_S1)
    wqT = _tr(inputs["L0_attn_Wq"]); wkT = _tr(inputs["L0_attn_Wk"])
    wvT = _tr(inputs["L0_attn_Wv"])
    r1 = _run(nc, [dict(x=s, wqT=wqT, wkT=wkT, wvT=wvT,
                        bq=_f32(inputs["L0_attn_bq"])[None, :],
                        bk=_f32(inputs["L0_attn_bk"])[None, :],
                        bv=_f32(inputs["L0_attn_bv"])[None, :],
                        ident=ident_b) for s in shards(xf)])
    nc = get_prog("S2", build_S2)
    woT = _tr(inputs["L0_attn_Wo"])
    r2 = _run(nc, [dict(x=shards(xf)[c], qT=np.asarray(r1[c]["qT"]),
                        kvp0=np.asarray(r1[(c // 4) * 4 + 0]["kvp"]),
                        kvp1=np.asarray(r1[(c // 4) * 4 + 1]["kvp"]),
                        kvp2=np.asarray(r1[(c // 4) * 4 + 2]["kvp"]),
                        kvp3=np.asarray(r1[(c // 4) * 4 + 3]["kvp"]),
                        woT=woT, bo=_f32(inputs["L0_attn_bo"])[None, :], ident=ident_b)
                   for c in range(NC)])
    x1 = _gather(r2, "x1")
    if TRACE is not None:
        TRACE["x1"] = x1

    nc = get_prog("S3", build_S3)
    qw = np.asarray(inputs["L0_ffn_qweights"])
    qwT = np.ascontiguousarray(qw.transpose(0, 2, 1)).reshape(6 * D, D).astype(np.int32)
    scl = _f32(inputs["L0_ffn_scales"])
    screp = np.ascontiguousarray(
        np.repeat(scl, 16, axis=2).transpose(0, 2, 1)).reshape(6 * D, D)
    laT = np.ascontiguousarray(_f32(inputs["L0_ffn_lora_a"]).transpose(0, 2, 1)
                               ).reshape(6 * D, 32)
    lbT = np.ascontiguousarray(_f32(inputs["L0_ffn_lora_b"]).transpose(0, 2, 1)
                               ).reshape(6 * 32, D)
    r3 = _run(nc, [dict(x=s, qwT=qwT, screp=screp,
                        biases=_f32(inputs["L0_ffn_biases"]),
                        laT=laT, lbT=lbT,
                        lng=_f32(inputs["L0_ffn_ln_gamma"]),
                        lnb=_f32(inputs["L0_ffn_ln_beta"]), ident=ident_b)
                   for s in shards(x1)])
    x2 = _gather(r3, "x2")
    if TRACE is not None:
        TRACE["x2"] = x2

    nc = get_prog("P30", build_P30)
    zeros = np.zeros((NT, D), np.float32)
    rp = _run(nc, [dict(a=s, b=zeros) for s in shards(x2)])
    ln2 = _gather(rp, "ln")
    nc = get_prog("S4", build_S4)
    r4 = _run(nc, [dict(ln=s, wqT=_tr(inputs["L1_attn_Wq"]),
                        bq=_f32(inputs["L1_attn_bq"])[None, :],
                        wvT=_tr(inputs["L1_attn_Wv"]),
                        bv=_f32(inputs["L1_attn_bv"])[None, :],
                        hyp=_f32(inputs["L1_attn_hyperplanes"]),
                        ident=ident_b, identf=ident_f) for s in shards(ln2)])
    qT_all = [np.concatenate([np.asarray(r4[b * 4 + q]["qT"]) for q in range(4)], axis=1)
              for b in range(B)]
    v_all = [np.concatenate([np.asarray(r4[b * 4 + q]["v"]) for q in range(4)], axis=1)
             for b in range(B)]
    bk_all = [np.concatenate([np.asarray(r4[b * 4 + q]["bkt"]).T for q in range(4)],
                             axis=1) for b in range(B)]
    nc = get_prog("S5", build_S5)
    ao = np.zeros((B, T, D), np.float32)
    for j in range(4):
        ims = []
        for c in range(NC):
            g = c * 4 + j
            b, h = g // 16, g % 16
            ims.append(dict(qTh=np.ascontiguousarray(qT_all[b][h * 64:(h + 1) * 64, :]),
                            vh=np.ascontiguousarray(v_all[b][h]),
                            bkh=np.ascontiguousarray(bk_all[b][h])[:, None],
                            ident=ident_b))
        rr = _run(nc, ims)
        for c in range(NC):
            g = c * 4 + j
            b, h = g // 16, g % 16
            ao[b, :, h * 64:(h + 1) * 64] = np.asarray(rr[c]["ao"])
    ao = ao.reshape(NTOK, D)
    nc = get_prog("P30", build_P30)
    rp = _run(nc, [dict(a=shards(x2)[c], b=shards(ao)[c]) for c in range(NC)])
    x3 = _gather(rp, "s")
    h_cur = _gather(rp, "ln")
    if TRACE is not None:
        TRACE["x3"] = x3
        TRACE["h0"] = h_cur

    cs, ss, ci, si = _rotary_tables()
    masks4 = _l1_masks()
    Wqkv = _f32(inputs["L1_ffn_Wqkv"]); Wo1 = _f32(inputs["L1_ffn_Wo"])
    W1 = _f32(inputs["L1_ffn_W1"]); W2 = _f32(inputs["L1_ffn_W2"])
    b1 = _f32(inputs["L1_ffn_b1"]); b2 = _f32(inputs["L1_ffn_b2"])
    lg = _f32(inputs["L1_ffn_ln_g"]); lb = _f32(inputs["L1_ffn_ln_b"])
    qsc = _f32(inputs["L1_ffn_q_scale"]); ksc = _f32(inputs["L1_ffn_k_scale"])
    ncA = get_prog("S7", build_S7)
    ncF = get_prog("S8", build_S8)
    for layer in range(6):
        ims = []
        for c in range(NC):
            t0 = (c % 4) * NT
            lo = t0 - 256
            tabrows = []
            for tab in (cs, ss, ci, si):
                if lo < 0:
                    tt = np.concatenate([np.zeros((256, 64), np.float32), tab[:t0 + NT]], 0)
                else:
                    tt = tab[lo:t0 + NT]
                tabrows.append(tt)
            tabs = np.ascontiguousarray(np.stack(tabrows, 0))
            msk = np.zeros((TIL, 128, 512), np.float32)
            for i in range(TIL):
                first = (c % 4 == 0) and (i < 2)
                msk[i] = masks4[(2 if first else 0) + (i % 2)]
            ims.append(dict(hh=_halo(h_cur, c, 257, True), wqkv=Wqkv[layer],
                            wo=Wo1[layer], qsc=qsc[layer][None, :], ksc=ksc[layer][None, :],
                            tabs=tabs, masks=msk, ident=ident_b))
        rr = _run(ncA, ims)
        h_cur = _gather(rr, "h")
        ims = [dict(hh=_halo(h_cur, c, 1, True), lng=lg[layer][None, :],
                    lnb=lb[layer][None, :], w1=W1[layer], b1=b1[layer][None, :],
                    w2=W2[layer], b2=b2[layer][None, :], ident=ident_b)
               for c in range(NC)]
        rr = _run(ncF, ims)
        h_cur = _gather(rr, "h")
        if TRACE is not None:
            TRACE[f"l1f_h{layer}"] = h_cur.copy()
    nc = get_prog("P30", build_P30)
    rp = _run(nc, [dict(a=shards(x3)[c], b=shards(h_cur)[c]) for c in range(NC)])
    x4 = _gather(rp, "s")
    ln4 = _gather(rp, "ln")
    if TRACE is not None:
        TRACE["x4"] = x4

    nc = get_prog("S9", build_S9)
    wq2 = _f32(inputs["L2_attn_Wq"]).reshape(D, D)
    wk2 = _f32(inputs["L2_attn_Wk"]).reshape(D, D)
    wv2 = _f32(inputs["L2_attn_Wv"]).reshape(D, D)
    r9 = _run(nc, [dict(ln=s, wq=wq2, wk=wk2, wv=wv2,
                        projT=_tr(inputs["L2_attn_proj"]),
                        ident=ident_b, identf=ident_f) for s in shards(ln4)])
    kmaxs = np.array([[float(np.asarray(r9[c]["kmax"])[0, 0]) for c in range(NC)]],
                     np.float32)
    nc = get_prog("S10", build_S10)
    r10 = _run(nc, [dict(kd=np.asarray(r9[c]["kd"]),
                         dgk=np.ascontiguousarray(np.asarray(r9[c]["diag"])[:, 16:]),
                         v=np.asarray(r9[c]["v"]), kmaxs=kmaxs, ident=ident_b)
                    for c in range(NC)])
    nc = get_prog("S11a", build_S11a)
    wo2 = np.ascontiguousarray(_f32(inputs["L2_attn_Wo"]).reshape(D, D))
    r11 = _run(nc, [dict(qd=np.asarray(r9[c]["qd"]),
                         dgq=np.ascontiguousarray(np.asarray(r9[c]["diag"])[:, :16]),
                         lnx=shards(ln4)[c],
                         kvp0=np.asarray(r10[(c // 4) * 4 + 0]["kvp"]),
                         kvp1=np.asarray(r10[(c // 4) * 4 + 1]["kvp"]),
                         kvp2=np.asarray(r10[(c // 4) * 4 + 2]["kvp"]),
                         kvp3=np.asarray(r10[(c // 4) * 4 + 3]["kvp"]),
                         wof=wo2,
                         g1=_f32(inputs["L2_attn_ln1_g"])[None, :],
                         be1=_f32(inputs["L2_attn_ln1_b"])[None, :],
                         ident=ident_b) for c in range(NC)])
    nc = get_prog("S11b", build_S11b)
    r11b = _run(nc, [dict(o1=np.asarray(r11[c]["o1"]), x4=shards(x4)[c],
                          w1=_f32(inputs["L2_attn_W1"]),
                          b1=_f32(inputs["L2_attn_b1"])[None, :],
                          w2=_f32(inputs["L2_attn_W2"]),
                          b2=_f32(inputs["L2_attn_b2"])[None, :],
                          g2=_f32(inputs["L2_attn_ln2_g"])[None, :],
                          be2=_f32(inputs["L2_attn_ln2_b"])[None, :],
                          ident=ident_b) for c in range(NC)])
    x5 = _gather(r11b, "x5")
    if TRACE is not None:
        TRACE["x5"] = x5

    nc = get_prog("P30", build_P30)
    rp = _run(nc, [dict(a=s, b=zeros) for s in shards(x5)])
    ln5 = _gather(rp, "ln")
    nc = get_prog("S12", build_S12)
    r12 = _run(nc, [dict(lnh=_halo(ln5, c, 1, True), x5=shards(x5)[c],
                         wrT=_tr(inputs["L2_ffn_W_r"]), wkT=_tr(inputs["L2_ffn_W_k"]),
                         wvT=_tr(inputs["L2_ffn_W_v"]),
                         mur=_f32(inputs["L2_ffn_mu_r"])[None, :],
                         muk=_f32(inputs["L2_ffn_mu_k"])[None, :], ident=ident_b)
                    for c in range(NC)])
    x6 = _gather(r12, "x6")
    if TRACE is not None:
        TRACE["x6"] = x6

    nc = get_prog("P30", build_P30)
    rp = _run(nc, [dict(a=s, b=zeros) for s in shards(x6)])
    ln6 = _gather(rp, "ln")
    nc = get_prog("S13", build_S13)
    r13 = _run(nc, [dict(ln=s, wzT=_tr(inputs["L3_attn_Wz"]),
                         whT=_tr(inputs["L3_attn_Wh"]),
                         bz=_f32(inputs["L3_attn_bz"])[None, :],
                         bh=_f32(inputs["L3_attn_bh"])[None, :],
                         ident=ident_b, identf=ident_f) for s in shards(ln6)])
    aT = np.zeros((B, 8, 128, T), np.float32)
    bT = np.zeros((B, 8, 128, T), np.float32)
    for c in range(NC):
        b, q = c // 4, c % 4
        aT[b, :, :, q * NT:(q + 1) * NT] = np.asarray(r13[c]["aT"])
        bT[b, :, :, q * NT:(q + 1) * NT] = np.asarray(r13[c]["bT"])
    nc = get_prog("S14", build_S14)
    ims = []
    for c in range(NC):
        g0, g1_ = 2 * c, 2 * c + 1
        ims.append(dict(aT=np.stack([aT[g0 // 8, g0 % 8], aT[g1_ // 8, g1_ % 8]]),
                        bT=np.stack([bT[g0 // 8, g0 % 8], bT[g1_ // 8, g1_ % 8]]),
                        identf=ident_f))
    r14 = _run(nc, ims)
    h_out = np.zeros((B, T, D), np.float32)
    for c in range(NC):
        for u in range(2):
            g = 2 * c + u
            b, k = g // 8, g % 8
            h_out[b, :, k * 128:(k + 1) * 128] = np.asarray(r14[c]["ho"])[u]
    h_out = h_out.reshape(NTOK, D)
    if TRACE is not None:
        TRACE["h_scan"] = h_out

    nc = get_prog("P30", build_P30)
    rp = _run(nc, [dict(a=shards(x6)[c], b=shards(h_out)[c]) for c in range(NC)])
    x7 = _gather(rp, "s")
    ln7 = _gather(rp, "ln")
    if TRACE is not None:
        TRACE["x7"] = x7
    nc = get_prog("L7a", build_L7a)
    ra = _run(nc, [dict(ln=s, wg=_f32(inputs["L3_ffn_Wg"]), ident=ident_b)
                   for s in shards(ln7)])
    wmat = _gather(ra, "w")
    acc = np.zeros((NTOK, D), np.float32)
    nc = get_prog("L7b", build_L7b)
    We1 = _f32(inputs["L3_ffn_We1"]); We3 = _f32(inputs["L3_ffn_We3"])
    We2 = _f32(inputs["L3_ffn_We2"])
    ones_col = np.ones((NT, 1), np.float32)
    passes = [(None, _f32(inputs["L3_ffn_Ws1"]), _f32(inputs["L3_ffn_Ws3"]),
               _f32(inputs["L3_ffn_Ws2"]))]
    for e in range(8):
        passes.append((e, We1[e], We3[e], We2[e]))
    for (e, w1e, w3e, w2e) in passes:
        ims = []
        for c in range(NC):
            wcol = ones_col if e is None else \
                np.ascontiguousarray(wmat[c * NT:(c + 1) * NT, e:e + 1])
            ims.append(dict(ln=shards(ln7)[c], wcol=wcol, we1=w1e, we3=w3e, we2=w2e,
                            acc=shards(acc)[c], ident=ident_b))
        rr = _run(nc, ims)
        acc = _gather(rr, "acc_o")
    nc = get_prog("L7c", build_L7c)
    rc = _run(nc, [dict(x7=shards(x7)[c], acc=shards(acc)[c]) for c in range(NC)])
    out = _gather(rc, "o")
    return out.reshape(B, T, D).astype(np.float32)
